# version 22
# baseline (speedup 1.0000x reference)
import contextlib
import numpy as np
import ml_dtypes

import concourse.bass as bass
import concourse.mybir as mybir
from concourse.bass_utils import run_bass_kernel_spmd

F32 = mybir.dt.float32
BF16 = mybir.dt.bfloat16
I32 = mybir.dt.int32
AF = mybir.ActivationFunctionType
ALU = mybir.AluOpType
AX = mybir.AxisListType

NCORES = 8
D = 1024
NH = 16
HD = 64
DL = 256
FF = 3072
TPC = 4096
ST = 512
NST = TPC // ST
HF = 256
XC = D // 128
EXPC = 12.0
EPS = 1e-5
SCL_FF = 1.0 / (127.0 * 127.0)


class Sem:
    def __init__(self, ctx, nc, name):
        self.h = ctx.enter_context(nc.semaphore(name))
        self.c = 0

    def inc(self, inst, n=1):
        inst.then_inc(self.h, n)
        self.c += n
        return self.c


def build_nc(nst=NST):
    nc = bass.Bass("TRN2", target_bir_lowering=False)

    xt = nc.dram_tensor("xt", [D, TPC], F32, kind="ExternalInput")
    wq_d = nc.dram_tensor("wq", [128, 8192], F32, kind="ExternalInput")
    wd_d = nc.dram_tensor("wd", [128, 2048], F32, kind="ExternalInput")
    wupk_d = nc.dram_tensor("wupk", [128, 4096], F32, kind="ExternalInput")
    wupv_d = nc.dram_tensor("wupv", [128, 4096], F32, kind="ExternalInput")
    wout_d = nc.dram_tensor("wout", [128, 8192], F32, kind="ExternalInput")
    wg_d = nc.dram_tensor("wg", [128, 24576], I32, kind="ExternalInput")
    wu_d = nc.dram_tensor("wu", [128, 24576], I32, kind="ExternalInput")
    wdn_d = nc.dram_tensor("wdn", [128, 24576], I32, kind="ExternalInput")
    bq_d = nc.dram_tensor("bq", [128, 8], F32, kind="ExternalInput")
    bdn_d = nc.dram_tensor("bdn", [128, 2], F32, kind="ExternalInput")
    bupk_d = nc.dram_tensor("bupk", [128, 16], F32, kind="ExternalInput")
    bupv_d = nc.dram_tensor("bupv", [128, 16], F32, kind="ExternalInput")
    bout_d = nc.dram_tensor("bout", [128, 8], F32, kind="ExternalInput")
    selred_d = nc.dram_tensor("selred", [128, 128], F32, kind="ExternalInput")
    selz_d = nc.dram_tensor("selz", [16, 1024], F32, kind="ExternalInput")
    ones_d = nc.dram_tensor("onesm", [128, 128], F32, kind="ExternalInput")
    id_d = nc.dram_tensor("idm", [128, 128], F32, kind="ExternalInput")
    cvec_d = nc.dram_tensor("cvec", [128, 2], F32, kind="ExternalInput")
    yt = nc.dram_tensor("yt", [D, TPC], F32, kind="ExternalOutput")
    wq_s = nc.dram_tensor("wq_s", [128, 8192], BF16)
    wout_s = nc.dram_tensor("wout_s", [128, 8192], BF16)
    wg_s = nc.dram_tensor("wg_s", [128, 24576], BF16)
    wu_s = nc.dram_tensor("wu_s", [128, 24576], BF16)
    wdn_s = nc.dram_tensor("wdn_s", [128, 24576], BF16)

    ctx = contextlib.ExitStack()
    E = ctx.enter_context
    with ctx:
        wd_sb = E(nc.sbuf_tensor("wd_sb", [128, 2048], BF16))
        wupk_sb = E(nc.sbuf_tensor("wupk_sb", [128, 4096], BF16))
        wupv_sb = E(nc.sbuf_tensor("wupv_sb", [128, 4096], BF16))
        wqb = [E(nc.sbuf_tensor(f"wqb{i}", [128, 1024], BF16)) for i in range(2)]
        wob = [E(nc.sbuf_tensor(f"wob{i}", [128, 1024], BF16)) for i in range(2)]
        wgb = [E(nc.sbuf_tensor(f"wgb{i}", [128, 1024], BF16)) for i in range(2)]
        wub = [E(nc.sbuf_tensor(f"wub{i}", [128, 1024], BF16)) for i in range(2)]
        wdb = [E(nc.sbuf_tensor(f"wdb{i}", [128, 3072], BF16)) for i in range(2)]
        bq_sb = E(nc.sbuf_tensor("bq_sb", [128, 8], F32))
        bdn_sb = E(nc.sbuf_tensor("bdn_sb", [128, 2], F32))
        bupk_sb = E(nc.sbuf_tensor("bupk_sb", [128, 16], F32))
        bupv_sb = E(nc.sbuf_tensor("bupv_sb", [128, 16], F32))
        bout_sb = E(nc.sbuf_tensor("bout_sb", [128, 8], F32))
        cvec_sb = E(nc.sbuf_tensor("cvec_sb", [128, 2], F32))
        selred_sb = E(nc.sbuf_tensor("selred_sb", [128, 128], BF16))
        selz_sb = E(nc.sbuf_tensor("selz_sb", [16, 1024], BF16))
        ones_sb = E(nc.sbuf_tensor("ones_sb", [128, 128], BF16))
        id_sb = E(nc.sbuf_tensor("id_sb", [128, 128], BF16))
        x_bf = [E(nc.sbuf_tensor(f"x_bf{i}", [128, XC * ST], BF16))
                for i in range(2)]
        meanb1 = E(nc.sbuf_tensor("meanb1", [128, 512], F32))
        meanb2 = E(nc.sbuf_tensor("meanb2", [128, 512], F32))
        q_sb = E(nc.sbuf_tensor("q_sb", [128, XC * ST], BF16))
        lat_bf = E(nc.sbuf_tensor("lat_bf", [128, 2 * ST], BF16))
        krep = E(nc.sbuf_tensor("krep", [128, NH * HF], BF16))
        vdup_all = E(nc.sbuf_tensor("vdup_all", [128, NH * ST], BF16))
        e_sb = E(nc.sbuf_tensor("e_sb", [16, NH * HF], BF16))
        esum32 = E(nc.sbuf_tensor("esum32", [16, HF], F32))
        zinv = E(nc.sbuf_tensor("zinv", [16, HF], F32))
        zinvb = E(nc.sbuf_tensor("zinvb", [16, HF], BF16))
        p_sb = [E(nc.sbuf_tensor(f"p_sb{i}", [128, 512], BF16)) for i in range(2)]
        hadb = [E(nc.sbuf_tensor(f"hadb{i}", [128, XC * HF], BF16)) for i in range(2)]
        avn = E(nc.sbuf_tensor("avn", [128, XC * HF], BF16))
        r1_bf = E(nc.sbuf_tensor("r1_bf", [128, XC * ST], BF16))
        x1_bf = E(nc.sbuf_tensor("x1_bf", [128, XC * ST], BF16))
        r2_bf = E(nc.sbuf_tensor("r2_bf", [128, XC * ST], BF16))
        sqb = [E(nc.sbuf_tensor(f"sqb{i}", [128, 512], BF16)) for i in range(2)]
        musq = E(nc.sbuf_tensor("musq", [128, 512], F32))
        vpe = E(nc.sbuf_tensor("vpe", [128, 512], F32))
        rstd = E(nc.sbuf_tensor("rstd", [128, 512], F32))
        nmb = E(nc.sbuf_tensor("nmb", [128, 512], F32))
        lnscr = E(nc.sbuf_tensor("lnscr", [128, 512], F32))
        sqb2 = [E(nc.sbuf_tensor(f"sqb2{i}", [128, 512], BF16)) for i in range(2)]
        musq2 = E(nc.sbuf_tensor("musq2", [128, 512], F32))
        vpe2 = E(nc.sbuf_tensor("vpe2", [128, 512], F32))
        rstd2 = E(nc.sbuf_tensor("rstd2", [128, 512], F32))
        nmb2 = E(nc.sbuf_tensor("nmb2", [128, 512], F32))
        lnscr2 = E(nc.sbuf_tensor("lnscr2", [128, 512], F32))
        s_sil = [E(nc.sbuf_tensor(f"s_sil{i}", [128, 512], BF16)) for i in range(2)]
        P = [E(nc.psum_tensor(f"P{i}", [128, 512], F32)) for i in range(8)]

        s_ld = [Sem(ctx, nc, "s_lda"), Sem(ctx, nc, "s_ldb")]
        s_ld2 = [Sem(ctx, nc, "s_ld2a"), Sem(ctx, nc, "s_ld2b")]
        s_x = [Sem(ctx, nc, "s_xa"), Sem(ctx, nc, "s_xb"),
               Sem(ctx, nc, "s_xc"), Sem(ctx, nc, "s_xd")]
        s_gd = [Sem(ctx, nc, "s_gda"), Sem(ctx, nc, "s_gdb")]
        s_st = [Sem(ctx, nc, "s_sta"), Sem(ctx, nc, "s_stb")]
        s_out = [Sem(ctx, nc, "s_outa"), Sem(ctx, nc, "s_outb")]
        s_pe = Sem(ctx, nc, "s_pe")
        s_v = Sem(ctx, nc, "s_v")
        s_s = Sem(ctx, nc, "s_s")
        s_pe2 = Sem(ctx, nc, "s_pe2")
        s_v2 = Sem(ctx, nc, "s_v2")
        s_s2 = Sem(ctx, nc, "s_s2")

        prep_ctx = contextlib.ExitStack()
        stg = [prep_ctx.enter_context(nc.sbuf_tensor(f"stg{i}", [128, 1024], F32))
               for i in range(2)]
        stgb = [prep_ctx.enter_context(nc.sbuf_tensor(f"stgb{i}", [128, 1024], BF16))
                for i in range(2)]
        with nc.Block() as blk:
            @blk.sync
            def _(sync):
                srcs = [(bq_sb[:, :], bq_d[:, :]), (bdn_sb[:, :], bdn_d[:, :]),
                        (bupk_sb[:, :], bupk_d[:, :]), (bupv_sb[:, :], bupv_d[:, :]),
                        (bout_sb[:, :], bout_d[:, :]), (cvec_sb[:, :], cvec_d[:, :]),
                        (stg[0][:, 0:128], selred_d[:, :]),
                        (stg[0][:, 128:256], ones_d[:, :]),
                        (stg[0][:, 256:384], id_d[:, :]),
                        (stg[1][0:16, 0:1024], selz_d[:, :])]
                for j, (dst, src) in enumerate(srcs):
                    s_ld[j % 2].inc(sync.dma_start(dst, src), 16)
                sync.wait_ge(s_ld[0].h, s_ld[0].c)
                sync.wait_ge(s_ld[1].h, s_ld[1].c)

        with nc.Block() as blk:
            @blk.vector
            def _(vector):
                vector.tensor_copy(selred_sb[:, :], stg[0][:, 0:128])
                vector.tensor_copy(ones_sb[:, :], stg[0][:, 128:256])
                vector.tensor_copy(id_sb[:, :], stg[0][:, 256:384])
                vector.tensor_copy(selz_sb[:, :], stg[1][0:16, 0:1024])

        pieces = []
        for j in range(2):
            pieces.append((wd_d, j * 1024, F32, 'sb', wd_sb, j * 1024))
        for j in range(4):
            pieces.append((wupk_d, j * 1024, F32, 'sb', wupk_sb, j * 1024))
        for j in range(4):
            pieces.append((wupv_d, j * 1024, F32, 'sb', wupv_sb, j * 1024))
        for j in range(8):
            pieces.append((wq_d, j * 1024, F32, 'dr', wq_s, j * 1024))
        for j in range(8):
            pieces.append((wout_d, j * 1024, F32, 'dr', wout_s, j * 1024))
        for j in range(24):
            pieces.append((wg_d, j * 1024, I32, 'dr', wg_s, j * 1024))
        for j in range(24):
            pieces.append((wu_d, j * 1024, I32, 'dr', wu_s, j * 1024))
        for j in range(24):
            pieces.append((wdn_d, j * 1024, I32, 'dr', wdn_s, j * 1024))

        with nc.Block() as blk:
            ld0 = [s_ld[0].c, s_ld[1].c]
            st0 = [s_st[0].c, s_st[1].c]
            v0 = s_v.c
            st_cnt = [0, 0]
            st_at = {}
            for j, pc in enumerate(pieces):
                if pc[3] == 'dr':
                    st_cnt[j % 2] += 16
                    st_at[j] = st0[j % 2] + st_cnt[j % 2]

            @blk.sync
            def _(sync):
                for j, (src, soff, dt, kind, dst, doff) in enumerate(pieces):
                    if j >= 2:
                        sync.wait_ge(s_v.h, v0 + j - 1)
                    dstap = stg[j % 2][:, :]
                    if dt == I32:
                        dstap = dstap.bitcast(I32)
                    s_ld[j % 2].inc(sync.dma_start(dstap, src[:, soff:soff + 1024]), 16)

            @blk.vector
            def _(vector):
                for j, (src, soff, dt, kind, dst, doff) in enumerate(pieces):
                    vector.wait_ge(s_ld[j % 2].h, ld0[j % 2] + 16 * (j // 2 + 1))
                    srcap = stg[j % 2][:, :]
                    if dt == I32:
                        srcap = srcap.bitcast(I32)
                    if kind == 'sb':
                        s_v.inc(vector.tensor_copy(dst[:, doff:doff + 1024], srcap))
                    else:
                        if j - 2 in st_at:
                            vector.wait_ge(s_st[j % 2].h, st_at[j - 2])
                        s_v.inc(vector.tensor_copy(stgb[j % 2][:, :], srcap))

            @blk.gpsimd
            def _(gp):
                for j, (src, soff, dt, kind, dst, doff) in enumerate(pieces):
                    if kind == 'dr':
                        gp.wait_ge(s_v.h, v0 + j + 1)
                        s_st[j % 2].inc(gp.dma_start(dst[:, doff:doff + 1024],
                                                     stgb[j % 2][:, :]), 16)
                gp.wait_ge(s_st[0].h, s_st[0].c)
                gp.wait_ge(s_st[1].h, s_st[1].c)

        prep_ctx.close()
        h_sb = E(nc.sbuf_tensor("h_sb", [128, 24 * 512], BF16))
        x2b = [E(nc.sbuf_tensor(f"x2b{i}", [128, 512], F32)) for i in range(2)]
        import os as _os
        if _os.environ.get('SBUF_DBG'):
            print('SBUF remaining at peak:', nc.sbuf_bytes_remaining)

        SEM_BASE = {id(sm): sm.c for sm in
                    (s_ld + s_ld2 + s_x + s_gd + s_st + s_out +
                     [s_pe, s_v, s_s, s_pe2, s_v2, s_s2])}
        OPS = {k: [] for k in "TVSYG"}
        CH2 = {k: [] for k in "TVSYG"}
        CUR = [OPS]
        SEMOWN = {}
        DDEF = {'T': 260.0, 'V': 510.0, 'S': 700.0, 'Y': 60.0, 'G': 60.0}

        def add(stream, fn, waits=(), inc=None, n=1, d=None):
            ch = 'm' if CUR[0] is OPS else 'c'
            node = [tuple(waits), fn, inc, n, ch, stream,
                    DDEF[stream] if d is None else float(d), 0.0]
            CUR[0][stream].append(node)
            if inc is not None:
                for v in range(inc.c + 1, inc.c + n + 1):
                    SEMOWN[(id(inc), v)] = node
                inc.c += n
                return inc.c
            return None

        def dma(stream, psem, dst, src, waits=(), nbytes=262144):
            w = [(psem, psem.c)] + list(waits)
            add(stream, (lambda e, d=dst, sr=src: e.dma_start(d, sr)),
                waits=w, inc=psem, n=16, d=80.0)
            CUR[0][stream][-1][7] = nbytes * 0.01

        def flush_ch2():
            for k in "TVSYG":
                OPS[k].extend(CH2[k])
                CH2[k] = []

        ck = {
            'xcast_end': 0, 'wqb_rd': 0, 'wob_rd': 0, 'wdb_rd': 0, 'wgub_rd': 0,
            'kv_pe_end': 0, 'had_end': 0, 'avn_h1': 0,
            'erep_end': 0, 'wo_stt_end': 0, 'avtt_end': [0, 0], 'q_evac_last': 0,
            'musq1': 0, 'nm1': 0, 'musq2': 0, 'vpe2': 0, 'ffn': None,
            'nmb1_v': 0, 'nmb2_v': 0, 'wo_free': {0: None, 1: None},
            'vpesub1': 0, 'vpesub2': 0,
        }
        xc_store = {}

        def plan_head(s):
            t0 = s * ST
            xb = x_bf[s % 2]
            xcast_c = []
            for c in range(XC):
                w = []
                wfree = ck['wo_free'][s % 2]
                if wfree is not None:
                    w.append((s_v, wfree[c]))
                dst = xb[:, c * ST:(c + 1) * ST]
                src = xt[c * 128:(c + 1) * 128, t0:t0 + ST]
                w = [(s_x[c % 4], s_x[c % 4].c)] + w
                add('G', (lambda e, d=dst, sr=src: e.dma_start(d, sr)),
                    waits=w, inc=s_x[c % 4], n=16)
                CUR[0]['G'][-1][7] = 262144 * 0.01
                xcast_c.append((s_x[c % 4], s_x[c % 4].c))
            xc_store[s] = xcast_c

            pe0 = s_pe.c
            for m in range(2):
                for c in range(XC):
                    w = []
                    if m == 0:
                        w.append(xcast_c[c])
                        if c == 0:
                            if ck['avtt_end'][1]:
                                w.append((s_v, ck['avtt_end'][1]))
                    o = P[5 + m][:, :]
                    l = wd_sb[:, m * 1024 + c * 128:m * 1024 + (c + 1) * 128]
                    r = xb[:, c * ST:(c + 1) * ST]
                    add('T', (lambda e, o=o, l=l, r=r, st=(c == 0), sp=(c == XC - 1):
                              e.matmul(o, l, r, start=st, stop=sp)),
                        waits=w, inc=(s_pe if c == XC - 1 else None))
            lat_evac = []
            for m in range(2):
                w = [(s_pe, pe0 + m + 1)]
                if m == 0 and ck['kv_pe_end']:
                    w.append((s_pe, ck['kv_pe_end']))
                o, i_ = lat_bf[:, m * ST:(m + 1) * ST], P[5 + m][:, :]
                b = bdn_sb[:, m:m + 1]
                lat_evac.append(add('S', (lambda e, o=o, i=i_, b=b:
                                          e.activation(o, i, AF.Identity, bias=b)),
                                    waits=w, inc=s_s))
            lat_done = s_s.c

            ld0 = [s_ld[0].c, s_ld[1].c]
            pe0, sc0 = s_pe.c, s_s.c
            for m in range(XC):
                w = []
                if m < 2:
                    if ck['wqb_rd']:
                        w.append((s_pe, ck['wqb_rd']))
                else:
                    w.append((s_pe, pe0 + m - 1))
                dst, src = wqb[m % 2][:, :], wq_s[:, m * 1024:(m + 1) * 1024]
                dma('Y', s_ld[m % 2], dst, src, w)
            for m in range(XC):
                w = [(s_ld[m % 2], ld0[m % 2] + 16 * (m // 2 + 1))]
                w.append((s_s, lat_done if m < 2 else sc0 + m - 1))
                for c in range(XC):
                    o = P[5 + m % 2][:, :]
                    l = wqb[m % 2][:, c * 128:(c + 1) * 128]
                    r = xb[:, c * ST:(c + 1) * ST]
                    add('T', (lambda e, o=o, l=l, r=r, st=(c == 0), sp=(c == XC - 1):
                              e.matmul(o, l, r, start=st, stop=sp)),
                        waits=(w if c == 0 else ()), inc=(s_pe if c == XC - 1 else None))
            ck['wqb_rd'] = s_pe.c
            q_evac_v = []
            for m in range(XC):
                w = [(s_pe, pe0 + m + 1)]
                if m == 0 and ck['had_end']:
                    w.append((s_v, ck['had_end']))
                o, i_ = q_sb[:, m * ST:(m + 1) * ST], P[5 + m % 2][:, :]
                b = bq_sb[:, m:m + 1]
                q_evac_v.append(add('S', (lambda e, o=o, i=i_, b=b:
                                          e.activation(o, i, AF.Identity, bias=b)),
                                   waits=w, inc=s_s))
            ck['q_evac_last'] = q_evac_v[-1]
            return q_evac_v[-1]

        q_done = plan_head(0)

        def plan_ffn_ch2(s):
            t0 = s * ST
            f = ck['ffn']
            pe_b2, v_b2, s_b2 = f['pe2_prev'], f['v2_prev'], f['s2_prev']
            gd0 = [s_gd[0].c, s_gd[1].c]
            pe0, v0, sc0 = s_pe2.c, s_v2.c, s_s2.c
            for m in range(24):
                w = []
                if m < 2:
                    if ck['wgub_rd']:
                        w.append((s_pe2, ck['wgub_rd']))
                else:
                    w.append((s_pe2, pe0 + 2 * (m - 1)))
                d1, r1s = wgb[m % 2][:, :], wg_s[:, m * 1024:(m + 1) * 1024]
                d2, r2s = wub[m % 2][:, :], wu_s[:, m * 1024:(m + 1) * 1024]
                dma('G', s_gd[m % 2], d1, r1s, w)
                dma('G', s_gd[m % 2], d2, r2s)
            for m in range(24):
                w = [(s_gd[m % 2], gd0[m % 2] + 32 * (m // 2 + 1))]
                if m == 0:
                    w += [(s_pe2, pe_b2), (s_v2, v_b2), (s_s2, s_b2)]
                elif m == 1:
                    w.append((s_s2, s_b2))
                else:
                    w.append((s_s2, sc0 + m - 1))
                for c in range(XC):
                    o = P[0 if m % 2 == 0 else 4][:, :]
                    l = wgb[m % 2][:, c * 128:(c + 1) * 128]
                    r = x1_bf[:, c * ST:(c + 1) * ST]
                    ww = list(w) if c == 0 else []
                    if m == 0:
                        ww.append((s_v, f['x1_v'][c]))
                    add('T', (lambda e, o=o, l=l, r=r, st=(c == 0), sp=(c == XC - 1):
                              e.matmul(o, l, r, start=st, stop=sp)),
                        waits=ww, inc=(s_pe2 if c == XC - 1 else None))
                w2 = []
                if m >= 1:
                    w2.append((s_v2, v0 + m))
                for c in range(XC):
                    o = P[1][:, :]
                    l = wub[m % 2][:, c * 128:(c + 1) * 128]
                    r = x1_bf[:, c * ST:(c + 1) * ST]
                    add('T', (lambda e, o=o, l=l, r=r, st=(c == 0), sp=(c == XC - 1):
                              e.matmul(o, l, r, start=st, stop=sp)),
                        waits=(w2 if c == 0 else ()), inc=(s_pe2 if c == XC - 1 else None))
            ck['wgub_rd'] = s_pe2.c
            for m in range(24):
                w = [(s_pe2, pe0 + 2 * m + 1)]
                if m >= 2:
                    w.append((s_v2, v0 + m - 1))
                o, i_ = s_sil[m % 2][:, :], P[0 if m % 2 == 0 else 4][:, :]
                add('S', (lambda e, o=o, i=i_:
                          e.activation(o, i, AF.Silu, scale=1.0 / 127.0)),
                    waits=w, inc=s_s2, d=800)
            h_v = []
            for m in range(24):
                w = [(s_s2, sc0 + m + 1), (s_pe2, pe0 + 2 * m + 2)]
                o = h_sb[:, m * 512:(m + 1) * 512]
                a, b = s_sil[m % 2][:, :], P[1][:, :]
                h_v.append(add('V', (lambda e, o=o, a=a, b=b: e.tensor_mul(o, a, b)),
                               waits=w, inc=s_v2, d=780))

            ld0 = [s_ld2[0].c, s_ld2[1].c]
            pe1, v1 = s_pe2.c, s_v2.c
            for mo in range(XC):
                w = []
                if mo < 2:
                    if ck['wdb_rd']:
                        w.append((s_pe2, ck['wdb_rd']))
                else:
                    w.append((s_pe2, pe1 + mo - 1))
                dst, src = wdb[mo % 2][:, :], wdn_s[:, mo * 3072:(mo + 1) * 3072]
                dma('Y', s_ld2[mo % 2], dst, src, w, nbytes=786432)
            r2_v = []
            for mo in range(XC):
                w = [(s_ld2[mo % 2], ld0[mo % 2] + 16 * (mo // 2 + 1))]
                if mo >= 1:
                    w.append((s_v2, v1 + mo))
                for hc in range(24):
                    o = P[2][:, :]
                    l = wdb[mo % 2][:, hc * 128:(hc + 1) * 128]
                    r = h_sb[:, hc * 512:(hc + 1) * 512]
                    ww = list(w) if hc == 0 else []
                    if mo == 0:
                        ww.append((s_v2, h_v[hc]))
                    add('T', (lambda e, o=o, l=l, r=r, st=(hc == 0), sp=(hc == 23):
                              e.matmul(o, l, r, start=st, stop=sp)),
                        waits=ww, inc=(s_pe2 if hc == 23 else None))
            ck['wdb_rd'] = s_pe2.c
            for mo in range(XC):
                w = [(s_pe2, pe1 + mo + 1)]
                o = r2_bf[:, mo * ST:(mo + 1) * ST]
                i_ = P[2][:, :]
                x2_ = x1_bf[:, mo * ST:(mo + 1) * ST]
                r2_v.append(add('V', (lambda e, o=o, i=i_, x=x2_:
                                      e.scalar_tensor_tensor(o, i, SCL_FF, x,
                                                             ALU.mult, ALU.add)),
                                waits=w, inc=s_v2, d=810))

            pe2, v2, sc2 = s_pe2.c, s_v2.c, s_s2.c
            for J in range(XC):
                w = [(s_v2, r2_v[J])]
                if J >= 2:
                    w.append((s_pe2, pe2 + J - 1))
                o, i_ = sqb2[J % 2][:, :], r2_bf[:, J * ST:(J + 1) * ST]
                add('S', (lambda e, o=o, i=i_: e.activation(o, i, AF.Square)),
                    waits=w, inc=s_s2, d=850)
            for J in range(XC):
                w = [(s_s2, sc2 + J + 1)]
                if J == 0:
                    w.append((s_s, ck['musq1']))
                o2, r2 = P[3][:, :], sqb2[J % 2][:, :]
                add('T', (lambda e, o=o2, r=r2, st=(J == 0), sp=(J == XC - 1):
                          e.matmul(o, ones_sb[:, :], r, start=st, stop=sp)),
                    waits=w, inc=s_pe2)
            ex2_val = add('S', (lambda e, o=vpe2[:, :], i=P[3][:, :]:
                                e.activation(o, i, AF.Identity,
                                             scale=1.0 / 1024.0)),
                          waits=[(s_pe2, pe2 + XC), (s_v2, ck['vpesub2'])],
                          inc=s_s2)
            for J in range(XC):
                w = [(s_v2, r2_v[J])]
                if J == 0:
                    w.append((s_s2, ex2_val))
                o, r = P[3][:, :], r2_bf[:, J * ST:(J + 1) * ST]
                add('T', (lambda e, o=o, r=r, st=(J == 0), sp=(J == XC - 1):
                          e.matmul(o, ones_sb[:, :], r, start=st, stop=sp)),
                    waits=w, inc=s_pe2)
            mean2_val = add('S', (lambda e, o=meanb2[:, :], i=P[3][:, :]:
                                  e.activation(o, i, AF.Identity,
                                               scale=1.0 / 1024.0)),
                            waits=[(s_pe2, pe2 + 2 * XC), (s_v2, ck['nmb2_v'])],
                            inc=s_s2)
            musq_val = add('S', (lambda e, o=musq2[:, :], i=meanb2[:, :]:
                                 e.activation(o, i, AF.Square)),
                           waits=[(s_s2, mean2_val)], inc=s_s2)
            vpe2_val = add('V', (lambda e, o=vpe2[:, :], a=vpe2[:, :],
                                 b=musq2[:, :]: e.tensor_sub(o, a, b)),
                           waits=[(s_s2, musq_val), (s_v2, s_v2.c)],
                           inc=s_v2, d=640)
            ck['vpesub2'] = vpe2_val
            add('S', (lambda e, o=lnscr2[:, :], i=vpe2[:, :], b=cvec_sb[:, 1:2]:
                      e.activation(o, i, AF.Ln, bias=b)),
                waits=[(s_v2, vpe2_val)], inc=s_s2)
            rstd2_val = add('S', (lambda e, o=rstd2[:, :], i=lnscr2[:, :]:
                                  e.activation(o, i, AF.Exp, scale=-0.5)),
                            waits=[(s_s2, s_s2.c)], inc=s_s2)
            ck['nmb2_v'] = add('V', (lambda e, o=nmb2[:, :], i=meanb2[:, :],
                                     r=rstd2[:, :]:
                                     e.scalar_tensor_tensor(o, i, -1.0, r,
                                                            ALU.mult, ALU.mult)),
                               waits=[(s_s2, rstd2_val), (s_v2, s_v2.c)],
                               inc=s_v2, d=830)
            outb = [s_out[0].c, s_out[1].c]
            x2_v = []
            for J in range(XC):
                add('V', (lambda e, o=lnscr2[:, :], a=r2_bf[:, J * ST:(J + 1) * ST],
                          b=rstd2[:, :]: e.tensor_mul(o, a, b)),
                    waits=[(s_v2, s_v2.c)], inc=s_v2, d=830)
                w = [(s_v2, s_v2.c)]
                val = outb[J % 2] + 16 * (J // 2) if J >= 2 else outb[J % 2]
                if val > 0:
                    w.append((s_out[J % 2], val))
                o = x2b[J % 2][:, :]
                x2_v.append(add('V', (lambda e, o=o, a=lnscr2[:, :], b=nmb2[:, :]:
                                      e.tensor_add(o, a, b)),
                                waits=w, inc=s_v2, d=830))
            for J in range(XC):
                w = [(s_v2, x2_v[J])]
                dst = yt[J * 128:(J + 1) * 128, t0:t0 + ST]
                src = x2b[J % 2][:, :]
                dma('Y', s_out[J % 2], dst, src, w)
            ck['musq2'] = mean2_val

        for s in range(nst):
            if s >= 1:
                CUR[0] = CH2
                plan_ffn_ch2(s - 1)
                CUR[0] = OPS

            pe_b, v_b, s_b = ck.get('bar') or (s_pe.c, s_v.c, s_s.c)
            first_T, first_V, first_S = [True], [True], [True]

            def wT(extra=()):
                if first_T[0]:
                    first_T[0] = False
                    return ((s_v, v_b), (s_s, s_b)) + tuple(extra)
                return tuple(extra)

            def wV(extra=()):
                if first_V[0]:
                    first_V[0] = False
                    return ((s_pe, pe_b), (s_s, s_b), (s_v, v_b)) + tuple(extra)
                return tuple(extra)

            def wS(extra=()):
                if first_S[0]:
                    first_S[0] = False
                    return ((s_pe, pe_b), (s_v, v_b), (s_s, s_b)) + tuple(extra)
                return tuple(extra)

            prev_red_end = 0
            prev_av_pe_end = 0
            prev_avn_end = 0
            prev_esum_v = 0
            prev_wout_v = [0] * XC
            prev_avn_end_hf0 = 0

            def plan_kv_part(hf, part, guard=()):
                pe0, sc0 = s_pe.c, s_s.c
                wgt = wupk_sb
                for g in range(NH):
                    w = []
                    if g < 2:
                        w += list(wT([]))
                        w += list(guard)
                    if sc0 + g - 1 > 0:
                        w.append((s_s, sc0 + g - 1))
                    for c in range(2):
                        o = P[5 + g % 2][:, 0:HF]
                        l = wgt[:, g * 256 + c * 128:g * 256 + (c + 1) * 128]
                        r = lat_bf[:, c * ST + hf * HF:c * ST + hf * HF + HF]
                        add('T', (lambda e, o=o, l=l, r=r, st=(c == 0), sp=(c == 1):
                                  e.matmul(o, l, r, start=st, stop=sp)),
                            waits=(w if c == 0 else ()), inc=(s_pe if c == 1 else None), d=152)
                evs = []
                for g in range(NH):
                    dst = krep
                    bias = bupk_sb
                    w = list(wS([(s_pe, pe0 + g + 1)]))
                    if g == 0:
                        if prev_red_end:
                            w.append((s_pe, prev_red_end))
                    o, i_ = dst[:, g * HF:(g + 1) * HF], P[5 + g % 2][:, 0:HF]
                    b = bias[:, g:g + 1]
                    evs.append(add('S', (lambda e, o=o, i=i_, b=b:
                                         e.activation(o, i, AF.Identity, bias=b)),
                                   waits=tuple(w), inc=s_s, d=560))
                return evs

            def plan_vd():
                pe0, sc0 = s_pe.c, s_s.c
                for g in range(NH):
                    w = []
                    if sc0 + g - 1 > 0:
                        w.append((s_s, sc0 + g - 1))
                    for c in range(2):
                        o = P[5 + g % 2][:, 0:ST]
                        l = wupv_sb[:, g * 256 + c * 128:g * 256 + (c + 1) * 128]
                        r = lat_bf[:, c * ST:(c + 1) * ST]
                        add('T', (lambda e, o=o, l=l, r=r, st=(c == 0), sp=(c == 1):
                                  e.matmul(o, l, r, start=st, stop=sp)),
                            waits=(w if c == 0 else ()),
                            inc=(s_pe if c == 1 else None), d=258)
                evs = []
                for g in range(NH):
                    w = list(wS([(s_pe, pe0 + g + 1)]))
                    if g == 0:
                        for h2 in range(2):
                            if ck['avtt_end'][h2]:
                                w.append((s_v, ck['avtt_end'][h2]))
                    o = vdup_all[:, g * ST:(g + 1) * ST]
                    i_ = P[5 + g % 2][:, 0:ST]
                    b = bupv_sb[:, g:g + 1]
                    evs.append(add('S', (lambda e, o=o, i=i_, b=b:
                                         e.activation(o, i, AF.Identity, bias=b)),
                                   waits=tuple(w), inc=s_s, d=770))
                return evs

            def plan_sc(hf, kv_evac_g):
                nonlocal prev_red_end
                h0 = hf * HF
                pe0, v0, sc0 = s_pe.c, s_v.c, s_s.c
                q3 = q_sb[0:128, 0:XC * ST].rearrange("p (J t) -> p J t", J=XC)[:, :, h0:h0 + HF]
                for g in range(NH):
                    w = list(wV([]))
                    w.append((s_s, kv_evac_g[g]))
                    if g < 2:
                        if prev_red_end:
                            w.append((s_pe, prev_red_end))
                    else:
                        w.append((s_pe, pe0 + XC * (g - 1)))
                    o = hadb[g % 2][:, :].rearrange("p (J t) -> p J t", J=XC)
                    b = krep[:, g * HF:(g + 1) * HF].unsqueeze(1).broadcast_to([128, XC, HF])
                    add('V', (lambda e, o=o, a=q3, b=b: e.tensor_mul(o, a, b)),
                        waits=tuple(w), inc=s_v, d=1460)
                had_last = s_v.c
                for g in range(NH):
                    w = list(wS([(s_pe, pe0 + XC * (g + 1))]))
                    if g == 0 and prev_av_pe_end:
                        w.append((s_pe, prev_av_pe_end))
                        w.append((s_v, prev_esum_v))
                    o = e_sb[:, g * HF:(g + 1) * HF]
                    i_ = P[5 + g % 2][0:16, 0:HF]
                    b = cvec_sb[0:16, 0:1]
                    add('S', (lambda e, o=o, i=i_, b=b:
                              e.activation(o, i, AF.Exp, bias=b, scale=0.125)),
                        waits=tuple(w), inc=s_s, d=560)
                for g in range(NH):
                    for J in range(XC):
                        idx = g * XC + J
                        w = list(wT([])) if idx == 0 else []
                        if J == 0:
                            w.append((s_v, v0 + g + 1))
                            w.append((s_s, sc0 + g - 1 if g >= 2 else sc0))
                        o = P[5 + g % 2][0:16, 0:HF]
                        l = selred_sb[:, J * 16:(J + 1) * 16]
                        r = hadb[g % 2][:, J * HF:(J + 1) * HF]
                        add('T', (lambda e, o=o, l=l, r=r, st=(J == 0), sp=(J == XC - 1):
                                  e.matmul(o, l, r, start=st, stop=sp)),
                            waits=tuple(w), inc=s_pe, d=152)
                prev_red_end = s_pe.c
                return s_s.c, had_last

            def plan_av(hf, exp_end, kv_evac_g, guard=()):
                nonlocal prev_av_pe_end, prev_avn_end, prev_esum_v, prev_avn_end_hf0
                pe0, v0, sc0 = s_pe.c, s_v.c, s_s.c
                AVSCHED = ([('b', 0), ('b', 1), ('b', 2), ('g', 0), ('g', 1),
                            ('b', 3), ('g', 2), ('g', 3), ('b', 4), ('b', 5),
                            ('g', 4), ('g', 5), ('g', 6), ('g', 7),
                            ('b', 6), ('b', 7)] +
                           [('g', g) for g in range(8, NH)])
                pos_b, pos_g = {}, {}
                for k2, (kind2, idx2_) in enumerate(AVSCHED, 1):
                    (pos_b if kind2 == 'b' else pos_g)[idx2_] = k2
                pe_bcast = lambda J, gp: pe0 + J * 24 + pos_b[gp]
                pe_gsum = lambda J, g: pe0 + J * 24 + pos_g[g]
                NPRE = 25
                v_norm0 = v0 + 17
                v_norm = v0 + NPRE
                v_tt = lambda J, gp: v0 + NPRE + J * 8 + gp + 1
                avn_s = lambda J: sc0 + J + 1

                exp_g = lambda g: exp_end - NH + g + 1
                add('V', (lambda e, o=esum32[:, :], a=e_sb[:, 0:HF],
                          b=e_sb[:, HF:2 * HF]: e.tensor_tensor(o, a, b, ALU.add)),
                    waits=wV([(s_s, exp_g(1)), (s_v, s_v.c)]), inc=s_v, d=420)
                for g in range(2, NH):
                    a2 = e_sb[:, g * HF:(g + 1) * HF]
                    add('V', (lambda e, o=esum32[:, :], a=esum32[:, :], b=a2:
                              e.tensor_tensor(o, a, b, ALU.add)),
                        waits=[(s_s, exp_g(g)), (s_v, s_v.c)], inc=s_v, d=420)
                add('V', (lambda e, o=zinv[:, :], i=esum32[:, :]: e.reciprocal(o, i)),
                    waits=[(s_v, s_v.c)], inc=s_v, d=2100)
                add('V', (lambda e, o=zinvb[:, :], i=zinv[:, :]: e.tensor_copy(o, i)),
                    waits=[(s_v, s_v.c)], inc=s_v, d=370)
                for gp in range(8):
                    e3 = e_sb[:, gp * 512:(gp + 1) * 512].rearrange(
                        "p (g t) -> p g t", g=2)
                    zb = zinvb[0:16, 0:HF].unsqueeze(1).broadcast_to([16, 2, HF])
                    add('V', (lambda e, o=e3, a=e3, b=zb: e.tensor_mul(o, a, b)),
                        waits=[(s_v, s_v.c)], inc=s_v, d=440)
                assert s_v.c == v0 + NPRE
                prev_esum_v = s_v.c
                for J in range(XC):
                    for gp in range(8):
                        w = [(s_pe, pe_bcast(J, gp))]
                        if J == 0:
                            w.append((s_s, kv_evac_g[2 * gp + 1]))
                        if gp >= 2:
                            w.append((s_pe, pe_gsum(J, 2 * gp - 3)))
                        elif J >= 1:
                            w.append((s_pe, pe_gsum(J - 1, 13 + 2 * gp)))
                        o = p_sb[gp % 2][:, :].rearrange("p (g t) -> p g t", g=2)
                        a = P[5 + gp % 2][:, :].rearrange("p (g t) -> p g t", g=2)
                        b = vdup_all[0:128, 0:NH * ST].rearrange(
                            "p (g t) -> p g t", g=NH)[:, 2 * gp:2 * gp + 2,
                                                      hf * HF:hf * HF + HF]
                        add('V', (lambda e, o=o, a=a, b=b: e.tensor_mul(o, a, b)),
                            waits=w, inc=s_v, d=640)
                    o = avn[:, J * HF:(J + 1) * HF]
                    i_ = P[7][:, 0:HF]
                    add('S', (lambda e, o=o, i=i_:
                              e.activation(o, i, AF.Identity)),
                        waits=[(s_pe, pe_gsum(J, NH - 1)), (s_s, s_s.c)],
                        inc=s_s, d=510)
                ck['avtt_end'][hf] = v_tt(XC - 1, 7)
                prev_avn_end = s_s.c
                erep_end = v_tt(XC - 1, 7)
                for J in range(XC):
                    first = True
                    for kind, idx2 in AVSCHED:
                        w = []
                        if first:
                            first = False
                            if J == 0:
                                w += list(wT([]))
                                w.append((s_s, exp_end))
                                w += list(guard)
                        if kind == 'b' and J == 0:
                            w.append((s_v, v_norm0 + idx2 + 1))
                        if kind == 'b':
                            gp = idx2
                            lin = J * 8 + gp
                            if lin >= 2:
                                J2, gp2 = divmod(lin - 2, 8)
                                w.append((s_v, v_tt(J2, gp2)))
                            o = P[5 + gp % 2][:, :]
                            l = selz_sb[:, J * 128:(J + 1) * 128]
                            r = e_sb[:, gp * 512:(gp + 1) * 512]
                            add('T', (lambda e, o=o, l=l, r=r:
                                      e.matmul(o, l, r, start=True, stop=True)),
                                waits=w, inc=s_pe)
                        else:
                            g = idx2
                            if g % 2 == 0:
                                w.append((s_v, v_tt(J, g // 2)))
                            if g == 0:
                                if J >= 1:
                                    w.append((s_s, avn_s(J - 1)))
                                elif hf == 0:
                                    w.append((s_s, kv_evac_g[-1]))
                                else:
                                    w.append((s_s, prev_avn_end_hf0))
                            o = P[7][:, 0:HF]
                            r = p_sb[(g // 2) % 2][:, (g % 2) * HF:(g % 2 + 1) * HF]
                            add('T', (lambda e, o=o, r=r, st=(g == 0), sp=(g == NH - 1):
                                      e.matmul(o, id_sb[:, :], r, start=st, stop=sp)),
                                waits=w, inc=s_pe, d=152)
                assert s_pe.c == pe0 + XC * 24
                assert s_v.c == v0 + NPRE + XC * 8
                prev_av_pe_end = s_pe.c
                if hf == 0:
                    prev_avn_end_hf0 = prev_avn_end
                return prev_avn_end, erep_end

            def plan_wo(hf, avn_end, p56_guard):
                nonlocal prev_wout_v
                h0 = hf * HF
                ld0 = [s_ld[0].c, s_ld[1].c]
                pe0, v0 = s_pe.c, s_v.c
                for m in range(XC):
                    w = []
                    if m < 2:
                        if ck['wob_rd']:
                            w.append((s_pe, ck['wob_rd']))
                    else:
                        w.append((s_pe, pe0 + m - 1))
                    dst, src = wob[m % 2][:, :], wout_s[:, m * 1024:(m + 1) * 1024]
                    dma('Y', s_ld[m % 2], dst, src, w)
                for m in range(XC):
                    w = [(s_ld[m % 2], ld0[m % 2] + 16 * (m // 2 + 1))]
                    if m == 0:
                        w.append((s_s, avn_end))
                        w += p56_guard
                    if m < 2:
                        if prev_wout_v[0]:
                            w.append((s_v, prev_wout_v[XC - 2 + m]))
                    if m >= 2:
                        w.append((s_v, v0 + m - 1))
                    for J in range(XC):
                        o = P[5 + m % 2][:, 0:HF]
                        l = wob[m % 2][:, J * 128:(J + 1) * 128]
                        r = avn[:, J * HF:(J + 1) * HF]
                        add('T', (lambda e, o=o, l=l, r=r, st=(J == 0), sp=(J == XC - 1):
                                  e.matmul(o, l, r, start=st, stop=sp)),
                            waits=(w if J == 0 else ()), inc=(s_pe if J == XC - 1 else None))
                ck['wob_rd'] = s_pe.c
                wout_v = []
                for m in range(XC):
                    w = list(wV([(s_pe, pe0 + m + 1)]))
                    if m == 0 and hf == 0 and ck['musq1']:
                        w.append((s_s, ck['musq1']))
                    o = r1_bf[:, m * ST + h0:m * ST + h0 + HF]
                    i_ = P[5 + m % 2][:, 0:HF]
                    b = bout_sb[:, m:m + 1]
                    x2_ = x_bf[s % 2][:, m * ST + h0:m * ST + h0 + HF]
                    wout_v.append(add('V', (lambda e, o=o, i=i_, b=b, x=x2_:
                                            e.scalar_tensor_tensor(o, i, b, x,
                                                                   ALU.add, ALU.add)),
                                      waits=w, inc=s_v))
                prev_wout_v = wout_v
                return wout_v

            kd0 = plan_kv_part(0, 0)
            vd_all = plan_vd()
            exp_end0, had0 = plan_sc(0, kd0)
            avn_end0, erep_end0 = plan_av(0, exp_end0, vd_all)
            kd1 = plan_kv_part(1, 0, guard=[(s_v, ck['avtt_end'][0])])
            exp_end1, had1 = plan_sc(1, kd1)
            wo0 = plan_wo(0, avn_end0, [(s_s, exp_end1)])
            avn_end1, erep_end1 = plan_av(1, exp_end1, vd_all, [(s_v, wo0[-1])])
            wo1 = plan_wo(1, avn_end1, [(s_v, erep_end1)])
            ck['kv_pe_end'] = s_pe.c
            ck['had_end'] = had1
            ck['avn_h1'] = avn_end1
            ck['erep_end'] = erep_end1
            ck['wo_stt_end'] = wo1[-1]
            ck['wo_free'][s % 2] = [max(a, b) for a, b in zip(wo0, wo1)]

            flush_ch2()

            ck['bar'] = (s_pe.c, s_v.c, s_s.c)

            pe0, v0, sc0 = s_pe.c, s_v.c, s_s.c
            for J in range(XC):
                w = list(wS([]))
                w.append((s_v, wo1[J]))
                if J >= 2:
                    w.append((s_pe, pe0 + J - 1))
                elif ck['musq1']:
                    w.append((s_s, ck['musq1']))
                o, i_ = sqb[J % 2][:, :], r1_bf[:, J * ST:(J + 1) * ST]
                add('S', (lambda e, o=o, i=i_: e.activation(o, i, AF.Square)),
                    waits=tuple(w), inc=s_s, d=850)
            for J in range(XC):
                w = list(wT([]))
                w.append((s_s, sc0 + J + 1))
                if J == 0:
                    w.append((s_s2, ck['musq2']))
                o2, r2 = P[3][:, :], sqb[J % 2][:, :]
                add('T', (lambda e, o=o2, r=r2, st=(J == 0), sp=(J == XC - 1):
                          e.matmul(o, ones_sb[:, :], r, start=st, stop=sp)),
                    waits=tuple(w), inc=s_pe)
            ex1_val = add('S', (lambda e, o=vpe[:, :], i=P[3][:, :]:
                                e.activation(o, i, AF.Identity,
                                             scale=1.0 / 1024.0)),
                          waits=[(s_pe, pe0 + XC), (s_v, ck['vpesub1'])],
                          inc=s_s)
            for J in range(XC):
                w = [(s_v, wo1[J])]
                if J == 0:
                    w.append((s_s, ex1_val))
                o, r = P[3][:, :], r1_bf[:, J * ST:(J + 1) * ST]
                add('T', (lambda e, o=o, r=r, st=(J == 0), sp=(J == XC - 1):
                          e.matmul(o, ones_sb[:, :], r, start=st, stop=sp)),
                    waits=w, inc=s_pe)
            mean1_val = add('S', (lambda e, o=meanb1[:, :], i=P[3][:, :]:
                                  e.activation(o, i, AF.Identity,
                                               scale=1.0 / 1024.0)),
                            waits=[(s_pe, pe0 + 2 * XC), (s_v, ck['nmb1_v'])],
                            inc=s_s)
            musq_val = add('S', (lambda e, o=musq[:, :], i=meanb1[:, :]:
                                 e.activation(o, i, AF.Square)),
                           waits=[(s_s, mean1_val)], inc=s_s)
            vpe1_val = add('V', (lambda e, o=vpe[:, :], a=vpe[:, :],
                                 b=musq[:, :]: e.tensor_sub(o, a, b)),
                           waits=wV([(s_s, musq_val), (s_v, s_v.c)]),
                           inc=s_v, d=640)
            ck['vpesub1'] = vpe1_val
            add('S', (lambda e, o=lnscr[:, :], i=vpe[:, :], b=cvec_sb[:, 1:2]:
                      e.activation(o, i, AF.Ln, bias=b)),
                waits=[(s_v, vpe1_val)], inc=s_s)
            rstd1_val = add('S', (lambda e, o=rstd[:, :], i=lnscr[:, :]:
                                  e.activation(o, i, AF.Exp, scale=-0.5)),
                            waits=[(s_s, s_s.c)], inc=s_s)
            nmb1_v = add('V', (lambda e, o=nmb[:, :], i=meanb1[:, :],
                               r=rstd[:, :]:
                               e.scalar_tensor_tensor(o, i, -1.0, r,
                                                      ALU.mult, ALU.mult)),
                         waits=[(s_s, rstd1_val), (s_v, s_v.c)],
                         inc=s_v, d=830)
            ck['nmb1_v'] = nmb1_v
            ck['musq1'] = mean1_val
            x1_v = []
            for J in range(XC):
                add('V', (lambda e, o=lnscr[:, :], a=r1_bf[:, J * ST:(J + 1) * ST],
                          b=rstd[:, :]: e.tensor_mul(o, a, b)),
                    waits=[(s_v, s_v.c), (s_v, nmb1_v)], inc=s_v, d=830)
                o = x1_bf[:, J * ST:(J + 1) * ST]
                x1_v.append(add('V', (lambda e, o=o, a=lnscr[:, :], b=nmb[:, :]:
                                      e.tensor_add(o, a, b)),
                                waits=[(s_v, s_v.c)], inc=s_v, d=830))
            ck['nm1'] = vpe1_val

            ck['ffn'] = dict(x1_v=x1_v, pe2_prev=s_pe2.c, v2_prev=s_v2.c,
                             s2_prev=s_s2.c)

            if s + 1 < nst:
                plan_head(s + 1)

        CUR[0] = CH2
        plan_ffn_ch2(nst - 1)
        CUR[0] = OPS
        flush_ch2()

        add('Y', (lambda e: e.nop()),
            waits=[(s_out[0], s_out[0].c), (s_out[1], s_out[1].c)], d=30)

        nodes = [nd for k in "TVSYG" for nd in OPS[k]]
        nid = {id(nd): nd for nd in nodes}
        chains = {}
        for k in "TVSYG":
            for nd in OPS[k]:
                chains.setdefault((k, nd[4]), []).append(nd)
        preds = {id(nd): [] for nd in nodes}
        for nd in nodes:
            for sem, val in nd[0]:
                assert val <= sem.c, (nd[5], nd[4], sem.h.name, val, sem.c)
                t = SEMOWN.get((id(sem), val))
                if t is not None and id(t) in nid and t is not nd:
                    preds[id(nd)].append(t)
        succs = {id(nd): [] for nd in nodes}
        indeg = {id(nd): 0 for nd in nodes}
        for nd in nodes:
            for p in preds[id(nd)]:
                succs[id(p)].append(nd); indeg[id(nd)] += 1
        for lst in chains.values():
            for a, b in zip(lst, lst[1:]):
                succs[id(a)].append(b); indeg[id(b)] += 1
        topo = [nd for nd in nodes if indeg[id(nd)] == 0]
        qi = 0
        while qi < len(topo):
            nd = topo[qi]; qi += 1
            for sc in succs[id(nd)]:
                indeg[id(sc)] -= 1
                if indeg[id(sc)] == 0:
                    topo.append(sc)
        if len(topo) != len(nodes):
            done = {id(nd) for nd in topo}
            msgs = []
            for k in "TVSYG":
                cnt_k = 0
                for ci, nd in enumerate(OPS[k]):
                    if id(nd) in done or cnt_k > 3:
                        continue
                    cnt_k += 1
                    unres = [(sm.h.name, v) for sm, v in nd[0]
                             if (t := SEMOWN.get((id(sm), v))) is not None
                             and id(t) in nid and id(t) not in done]
                    msgs.append((k, nd[4], ci, unres))
            raise RuntimeError(
                f"dep cycle: {len(nodes)-len(topo)} stuck; first: {msgs}")
        CP = {id(nd): 0.0 for nd in nodes}
        for nd in reversed(topo):
            m = 0.0
            for sc in succs[id(nd)]:
                if CP[id(sc)] > m:
                    m = CP[id(sc)]
            CP[id(nd)] = m + nd[6] + nd[7]
        qfree = {k: 0.0 for k in "TVSYG"}
        cptr = {key: 0 for key in chains}
        finish = {}
        sched = {k: [] for k in "TVSYG"}
        todo = len(nodes)
        while todo:
            best = None
            for (k, ch), lst in chains.items():
                i = cptr[(k, ch)]
                if i >= len(lst):
                    continue
                nd = lst[i]
                es = qfree[k]
                ok = True
                for p in preds[id(nd)]:
                    f = finish.get(id(p))
                    if f is None:
                        ok = False
                        break
                    if f > es:
                        es = f
                if not ok:
                    continue
                key = (round(es / 800.0), -CP[id(nd)], es)
                if best is None or key < best[0]:
                    best = (key, k, ch, nd, es)
            if best is None:
                raise RuntimeError("scheduler deadlock")
            _, k, ch, nd, es = best
            qfree[k] = es + nd[6]
            finish[id(nd)] = es + nd[6] + nd[7]
            sched[k].append(nd)
            cptr[(k, ch)] += 1
            todo -= 1
        all_sems = (s_ld + s_ld2 + s_x + s_gd + s_st + s_out +
                    [s_pe, s_v, s_s, s_pe2, s_v2, s_s2])
        cnt = {id(sm): SEM_BASE[id(sm)] for sm in all_sems}
        semq = {}
        newval = {}
        for k in "TVSYG":
            for nd in sched[k]:
                if nd[2] is not None:
                    sid = id(nd[2])
                    assert semq.setdefault(sid, k) == k, "sem on two queues"
                    cnt[sid] += nd[3]
                    newval[id(nd)] = cnt[sid]
        for nd in nodes:
            nw = {}
            for sem, val in nd[0]:
                t = SEMOWN.get((id(sem), val))
                v2 = newval[id(t)] if (t is not None and id(t) in nid) else val
                if nw.get(id(sem), (None, -1))[1] < v2:
                    nw[id(sem)] = (sem, v2)
            nd[0] = tuple(nw.values())

        with nc.Block() as blk:
            def runner(ops_list):
                def body(eng):
                    for waits, fn, incsem, n, _ch, _st, _d, _x in ops_list:
                        for sem, val in waits:
                            eng.wait_ge(sem.h, val)
                        inst = fn(eng)
                        if incsem is not None:
                            inst.then_inc(incsem.h, n)
                return body
            blk.tensor(runner(sched['T']))
            blk.vector(runner(sched['V']))
            blk.scalar(runner(sched['S']))
            blk.sync(runner(sched['Y']))
            blk.gpsimd(runner(sched['G']))

    return nc



def _lhsT_pieces(W):
    K, M = W.shape
    kc, mt = K // 128, M // 128
    return np.ascontiguousarray(
        W.reshape(kc, 128, mt, 128).transpose(1, 2, 0, 3).reshape(128, mt * kc * 128))


def kernel(L0_attn_x, L0_attn_Wq, L0_attn_bq, L0_attn_Wdown, L0_attn_bdown,
           L0_attn_Wup, L0_attn_bup, L0_attn_Wout, L0_attn_bout,
           L0_ffn_w_gate, L0_ffn_w_up, L0_ffn_w_down):
    x = np.asarray(L0_attn_x, dtype=np.float32).reshape(-1, D)
    Wq = np.asarray(L0_attn_Wq, dtype=np.float32)
    bq = np.asarray(L0_attn_bq, dtype=np.float32)
    Wd = np.asarray(L0_attn_Wdown, dtype=np.float32)
    bd = np.asarray(L0_attn_bdown, dtype=np.float32)
    Wu = np.asarray(L0_attn_Wup, dtype=np.float32)
    bu = np.asarray(L0_attn_bup, dtype=np.float32)
    Wo = np.asarray(L0_attn_Wout, dtype=np.float32)
    bo = np.asarray(L0_attn_bout, dtype=np.float32)
    wg = np.asarray(L0_ffn_w_gate, dtype=np.int32)
    wu = np.asarray(L0_ffn_w_up, dtype=np.int32)
    wdn = np.asarray(L0_ffn_w_down, dtype=np.int32)

    Wu3 = Wu.reshape(DL, NH, 2 * HD)
    kc_ = Wu3[:, :, :HD]
    vc_ = Wu3[:, :, HD:]
    wupk_dup = np.concatenate([kc_, kc_], axis=2).reshape(DL, NH * 128)
    wupv_dup = np.concatenate([vc_, vc_], axis=2).reshape(DL, NH * 128)

    buk = bu.reshape(NH, 2 * HD)[:, :HD]
    buv = bu.reshape(NH, 2 * HD)[:, HD:]
    bupk_h = np.ascontiguousarray(np.concatenate([buk, buk], axis=1).T)
    bupv_h = np.ascontiguousarray(np.concatenate([buv, buv], axis=1).T)

    selred = np.zeros((128, 128), dtype=np.float32)
    for J in range(XC):
        for p in range(128):
            selred[p, J * 16 + 2 * J + p // 64] = 1.0
    selz = np.zeros((16, 1024), dtype=np.float32)
    for J in range(XC):
        for hh in range(2):
            selz[2 * J + hh, J * 128 + hh * HD:J * 128 + (hh + 1) * HD] = 1.0
    cvec = np.zeros((128, 2), dtype=np.float32)
    cvec[:, 0] = -EXPC
    cvec[:, 1] = EPS

    common = dict(
        wq=_lhsT_pieces(Wq), wd=_lhsT_pieces(Wd),
        wupk=_lhsT_pieces(wupk_dup), wupv=_lhsT_pieces(wupv_dup),
        wout=_lhsT_pieces(Wo),
        wg=_lhsT_pieces(np.ascontiguousarray(wg.T)),
        wu=_lhsT_pieces(np.ascontiguousarray(wu.T)),
        wdn=_lhsT_pieces(np.ascontiguousarray(wdn.T)),
        bq=np.ascontiguousarray(bq.reshape(XC, 128).T),
        bdn=np.ascontiguousarray(bd.reshape(2, 128).T),
        bupk=bupk_h, bupv=bupv_h,
        bout=np.ascontiguousarray(bo.reshape(XC, 128).T),
        selred=selred, selz=selz, cvec=cvec,
        onesm=np.ones((128, 128), dtype=np.float32),
        idm=np.eye(128, dtype=np.float32),
    )
    in_maps = []
    for i in range(NCORES):
        m = dict(common)
        m["xt"] = np.ascontiguousarray(x[i * TPC:(i + 1) * TPC, :].T)
        in_maps.append(m)

    nc = build_nc()
    res = run_bass_kernel_spmd(nc, in_maps, core_ids=list(range(NCORES)))
    out = np.empty((NCORES * TPC, D), dtype=np.float32)
    for i in range(NCORES):
        out[i * TPC:(i + 1) * TPC, :] = res.results[i]["yt"].T
    return out.reshape(4, 8192, D)


# version 23
# speedup vs baseline: 1.0504x; 1.0504x over previous
import contextlib
import numpy as np
import ml_dtypes

import concourse.bass as bass
import concourse.mybir as mybir
from concourse.bass_utils import run_bass_kernel_spmd

F32 = mybir.dt.float32
BF16 = mybir.dt.bfloat16
I32 = mybir.dt.int32
AF = mybir.ActivationFunctionType
ALU = mybir.AluOpType
AX = mybir.AxisListType

NCORES = 8
D = 1024
NH = 16
HD = 64
DL = 256
FF = 3072
TPC = 4096
ST = 512
NST = TPC // ST
HF = 256
XC = D // 128
EXPC = 12.0
EPS = 1e-5
SCL_FF = 1.0 / (127.0 * 127.0)


class Sem:
    def __init__(self, ctx, nc, name):
        self.h = ctx.enter_context(nc.semaphore(name))
        self.c = 0

    def inc(self, inst, n=1):
        inst.then_inc(self.h, n)
        self.c += n
        return self.c


def build_nc(nst=NST):
    nc = bass.Bass("TRN2", target_bir_lowering=False)

    xt = nc.dram_tensor("xt", [D, TPC], F32, kind="ExternalInput")
    wq_d = nc.dram_tensor("wq", [128, 8192], F32, kind="ExternalInput")
    wd_d = nc.dram_tensor("wd", [128, 2048], F32, kind="ExternalInput")
    wupk_d = nc.dram_tensor("wupk", [128, 4096], F32, kind="ExternalInput")
    wupv_d = nc.dram_tensor("wupv", [128, 4096], F32, kind="ExternalInput")
    wout_d = nc.dram_tensor("wout", [128, 8192], F32, kind="ExternalInput")
    wg_d = nc.dram_tensor("wg", [128, 24576], I32, kind="ExternalInput")
    wu_d = nc.dram_tensor("wu", [128, 24576], I32, kind="ExternalInput")
    wdn_d = nc.dram_tensor("wdn", [128, 24576], I32, kind="ExternalInput")
    bq_d = nc.dram_tensor("bq", [128, 8], F32, kind="ExternalInput")
    bdn_d = nc.dram_tensor("bdn", [128, 2], F32, kind="ExternalInput")
    bupk_d = nc.dram_tensor("bupk", [128, 16], F32, kind="ExternalInput")
    bupv_d = nc.dram_tensor("bupv", [128, 16], F32, kind="ExternalInput")
    bout_d = nc.dram_tensor("bout", [128, 8], F32, kind="ExternalInput")
    selred_d = nc.dram_tensor("selred", [128, 128], F32, kind="ExternalInput")
    selz_d = nc.dram_tensor("selz", [16, 1024], F32, kind="ExternalInput")
    ones_d = nc.dram_tensor("onesm", [128, 128], F32, kind="ExternalInput")
    id_d = nc.dram_tensor("idm", [128, 128], F32, kind="ExternalInput")
    cvec_d = nc.dram_tensor("cvec", [128, 2], F32, kind="ExternalInput")
    yt = nc.dram_tensor("yt", [D, TPC], F32, kind="ExternalOutput")
    wq_s = nc.dram_tensor("wq_s", [128, 8192], BF16)
    wout_s = nc.dram_tensor("wout_s", [128, 8192], BF16)
    wg_s = nc.dram_tensor("wg_s", [128, 24576], BF16)
    wu_s = nc.dram_tensor("wu_s", [128, 24576], BF16)
    wdn_s = nc.dram_tensor("wdn_s", [128, 24576], BF16)

    ctx = contextlib.ExitStack()
    E = ctx.enter_context
    with ctx:
        wd_sb = E(nc.sbuf_tensor("wd_sb", [128, 2048], BF16))
        wupk_sb = E(nc.sbuf_tensor("wupk_sb", [128, 4096], BF16))
        wupv_sb = E(nc.sbuf_tensor("wupv_sb", [128, 4096], BF16))
        wqb = [E(nc.sbuf_tensor(f"wqb{i}", [128, 1024], BF16)) for i in range(2)]
        wob = [E(nc.sbuf_tensor(f"wob{i}", [128, 1024], BF16)) for i in range(2)]
        wgb = [E(nc.sbuf_tensor(f"wgb{i}", [128, 1024], BF16)) for i in range(2)]
        wub = [E(nc.sbuf_tensor(f"wub{i}", [128, 1024], BF16)) for i in range(2)]
        wdb = [E(nc.sbuf_tensor(f"wdb{i}", [128, 3072], BF16)) for i in range(2)]
        bq_sb = E(nc.sbuf_tensor("bq_sb", [128, 8], F32))
        bdn_sb = E(nc.sbuf_tensor("bdn_sb", [128, 2], F32))
        bupk_sb = E(nc.sbuf_tensor("bupk_sb", [128, 16], F32))
        bupv_sb = E(nc.sbuf_tensor("bupv_sb", [128, 16], F32))
        bout_sb = E(nc.sbuf_tensor("bout_sb", [128, 8], F32))
        cvec_sb = E(nc.sbuf_tensor("cvec_sb", [128, 2], F32))
        selred_sb = E(nc.sbuf_tensor("selred_sb", [128, 128], BF16))
        selz_sb = E(nc.sbuf_tensor("selz_sb", [16, 1024], BF16))
        ones_sb = E(nc.sbuf_tensor("ones_sb", [128, 128], BF16))
        id_sb = E(nc.sbuf_tensor("id_sb", [128, 128], BF16))
        x_bf = [E(nc.sbuf_tensor(f"x_bf{i}", [128, XC * ST], BF16))
                for i in range(2)]
        meanb1 = E(nc.sbuf_tensor("meanb1", [128, 512], F32))
        meanb2 = E(nc.sbuf_tensor("meanb2", [128, 512], F32))
        q_sb = E(nc.sbuf_tensor("q_sb", [128, XC * ST], BF16))
        lat_bf = E(nc.sbuf_tensor("lat_bf", [128, 2 * ST], BF16))
        krep = E(nc.sbuf_tensor("krep", [128, NH * HF], BF16))
        vdup_all = E(nc.sbuf_tensor("vdup_all", [128, NH * ST], BF16))
        e_sb = E(nc.sbuf_tensor("e_sb", [16, NH * HF], BF16))
        esum32 = E(nc.sbuf_tensor("esum32", [16, HF], F32))
        zinv = E(nc.sbuf_tensor("zinv", [16, HF], F32))
        zinvb = E(nc.sbuf_tensor("zinvb", [16, HF], BF16))
        p_sb = [E(nc.sbuf_tensor(f"p_sb{i}", [128, 512], BF16)) for i in range(2)]
        hadb = [E(nc.sbuf_tensor(f"hadb{i}", [128, XC * HF], BF16)) for i in range(2)]
        avn = E(nc.sbuf_tensor("avn", [128, XC * HF], BF16))
        r1_bf = E(nc.sbuf_tensor("r1_bf", [128, XC * ST], BF16))
        x1_bf = E(nc.sbuf_tensor("x1_bf", [128, XC * ST], BF16))
        r2_bf = E(nc.sbuf_tensor("r2_bf", [128, XC * ST], BF16))
        sqb = [E(nc.sbuf_tensor(f"sqb{i}", [128, 512], BF16)) for i in range(2)]
        musq = E(nc.sbuf_tensor("musq", [128, 512], F32))
        vpe = E(nc.sbuf_tensor("vpe", [128, 512], F32))
        rstd = E(nc.sbuf_tensor("rstd", [128, 512], F32))
        nmb = E(nc.sbuf_tensor("nmb", [128, 512], F32))
        lnscr = E(nc.sbuf_tensor("lnscr", [128, 512], F32))
        sqb2 = [E(nc.sbuf_tensor(f"sqb2{i}", [128, 512], BF16)) for i in range(2)]
        musq2 = E(nc.sbuf_tensor("musq2", [128, 512], F32))
        vpe2 = E(nc.sbuf_tensor("vpe2", [128, 512], F32))
        rstd2 = E(nc.sbuf_tensor("rstd2", [128, 512], F32))
        nmb2 = E(nc.sbuf_tensor("nmb2", [128, 512], F32))
        lnscr2 = E(nc.sbuf_tensor("lnscr2", [128, 512], F32))
        s_sil = [E(nc.sbuf_tensor(f"s_sil{i}", [128, 512], BF16)) for i in range(2)]
        P = [E(nc.psum_tensor(f"P{i}", [128, 512], F32)) for i in range(8)]

        s_ld = [Sem(ctx, nc, "s_lda"), Sem(ctx, nc, "s_ldb")]
        s_ld2 = [Sem(ctx, nc, "s_ld2a"), Sem(ctx, nc, "s_ld2b")]
        s_x = [Sem(ctx, nc, "s_xa"), Sem(ctx, nc, "s_xb"),
               Sem(ctx, nc, "s_xc"), Sem(ctx, nc, "s_xd")]
        s_gd = [Sem(ctx, nc, "s_gda"), Sem(ctx, nc, "s_gdb")]
        s_st = [Sem(ctx, nc, "s_sta"), Sem(ctx, nc, "s_stb")]
        s_out = [Sem(ctx, nc, "s_outa"), Sem(ctx, nc, "s_outb")]
        s_pe = Sem(ctx, nc, "s_pe")
        s_v = Sem(ctx, nc, "s_v")
        s_s = Sem(ctx, nc, "s_s")
        s_pe2 = Sem(ctx, nc, "s_pe2")
        s_v2 = Sem(ctx, nc, "s_v2")
        s_s2 = Sem(ctx, nc, "s_s2")

        prep_ctx = contextlib.ExitStack()
        stg = [prep_ctx.enter_context(nc.sbuf_tensor(f"stg{i}", [128, 1024], F32))
               for i in range(2)]
        stgb = [prep_ctx.enter_context(nc.sbuf_tensor(f"stgb{i}", [128, 1024], BF16))
                for i in range(2)]
        with nc.Block() as blk:
            @blk.sync
            def _(sync):
                srcs = [(bq_sb[:, :], bq_d[:, :]), (bdn_sb[:, :], bdn_d[:, :]),
                        (bupk_sb[:, :], bupk_d[:, :]), (bupv_sb[:, :], bupv_d[:, :]),
                        (bout_sb[:, :], bout_d[:, :]), (cvec_sb[:, :], cvec_d[:, :]),
                        (stg[0][:, 0:128], selred_d[:, :]),
                        (stg[0][:, 128:256], ones_d[:, :]),
                        (stg[0][:, 256:384], id_d[:, :]),
                        (stg[1][0:16, 0:1024], selz_d[:, :])]
                for j, (dst, src) in enumerate(srcs):
                    s_ld[j % 2].inc(sync.dma_start(dst, src), 16)
                sync.wait_ge(s_ld[0].h, s_ld[0].c)
                sync.wait_ge(s_ld[1].h, s_ld[1].c)

        with nc.Block() as blk:
            @blk.vector
            def _(vector):
                vector.tensor_copy(selred_sb[:, :], stg[0][:, 0:128])
                vector.tensor_copy(ones_sb[:, :], stg[0][:, 128:256])
                vector.tensor_copy(id_sb[:, :], stg[0][:, 256:384])
                vector.tensor_copy(selz_sb[:, :], stg[1][0:16, 0:1024])

        pieces = []
        for j in range(2):
            pieces.append((wd_d, j * 1024, F32, 'sb', wd_sb, j * 1024))
        for j in range(4):
            pieces.append((wupk_d, j * 1024, F32, 'sb', wupk_sb, j * 1024))
        for j in range(4):
            pieces.append((wupv_d, j * 1024, F32, 'sb', wupv_sb, j * 1024))
        for j in range(8):
            pieces.append((wq_d, j * 1024, F32, 'dr', wq_s, j * 1024))
        for j in range(8):
            pieces.append((wout_d, j * 1024, F32, 'dr', wout_s, j * 1024))
        for j in range(24):
            pieces.append((wg_d, j * 1024, I32, 'dr', wg_s, j * 1024))
        for j in range(24):
            pieces.append((wu_d, j * 1024, I32, 'dr', wu_s, j * 1024))
        for j in range(24):
            pieces.append((wdn_d, j * 1024, I32, 'dr', wdn_s, j * 1024))

        with nc.Block() as blk:
            ld0 = [s_ld[0].c, s_ld[1].c]
            st0 = [s_st[0].c, s_st[1].c]
            v0 = s_v.c
            st_cnt = [0, 0]
            st_at = {}
            for j, pc in enumerate(pieces):
                if pc[3] == 'dr':
                    st_cnt[j % 2] += 16
                    st_at[j] = st0[j % 2] + st_cnt[j % 2]

            @blk.sync
            def _(sync):
                for j, (src, soff, dt, kind, dst, doff) in enumerate(pieces):
                    if j >= 2:
                        sync.wait_ge(s_v.h, v0 + j - 1)
                    dstap = stg[j % 2][:, :]
                    if dt == I32:
                        dstap = dstap.bitcast(I32)
                    s_ld[j % 2].inc(sync.dma_start(dstap, src[:, soff:soff + 1024]), 16)

            @blk.vector
            def _(vector):
                for j, (src, soff, dt, kind, dst, doff) in enumerate(pieces):
                    vector.wait_ge(s_ld[j % 2].h, ld0[j % 2] + 16 * (j // 2 + 1))
                    srcap = stg[j % 2][:, :]
                    if dt == I32:
                        srcap = srcap.bitcast(I32)
                    if kind == 'sb':
                        s_v.inc(vector.tensor_copy(dst[:, doff:doff + 1024], srcap))
                    else:
                        if j - 2 in st_at:
                            vector.wait_ge(s_st[j % 2].h, st_at[j - 2])
                        s_v.inc(vector.tensor_copy(stgb[j % 2][:, :], srcap))

            @blk.gpsimd
            def _(gp):
                for j, (src, soff, dt, kind, dst, doff) in enumerate(pieces):
                    if kind == 'dr':
                        gp.wait_ge(s_v.h, v0 + j + 1)
                        s_st[j % 2].inc(gp.dma_start(dst[:, doff:doff + 1024],
                                                     stgb[j % 2][:, :]), 16)
                gp.wait_ge(s_st[0].h, s_st[0].c)
                gp.wait_ge(s_st[1].h, s_st[1].c)

        prep_ctx.close()
        h_sb = E(nc.sbuf_tensor("h_sb", [128, 24 * 512], BF16))
        x2b = [E(nc.sbuf_tensor(f"x2b{i}", [128, 512], F32)) for i in range(2)]
        import os as _os
        if _os.environ.get('SBUF_DBG'):
            print('SBUF remaining at peak:', nc.sbuf_bytes_remaining)

        SEM_BASE = {id(sm): sm.c for sm in
                    (s_ld + s_ld2 + s_x + s_gd + s_st + s_out +
                     [s_pe, s_v, s_s, s_pe2, s_v2, s_s2])}
        OPS = {k: [] for k in "TVSYG"}
        CH2 = {k: [] for k in "TVSYG"}
        CUR = [OPS]
        SEMOWN = {}
        DDEF = {'T': 260.0, 'V': 510.0, 'S': 700.0, 'Y': 60.0, 'G': 60.0}

        def add(stream, fn, waits=(), inc=None, n=1, d=None):
            ch = 'm' if CUR[0] is OPS else 'c'
            node = [tuple(waits), fn, inc, n, ch, stream,
                    DDEF[stream] if d is None else float(d), 0.0]
            CUR[0][stream].append(node)
            if inc is not None:
                for v in range(inc.c + 1, inc.c + n + 1):
                    SEMOWN[(id(inc), v)] = node
                inc.c += n
                return inc.c
            return None

        def dma(stream, psem, dst, src, waits=(), nbytes=262144):
            w = [(psem, psem.c)] + list(waits)
            add(stream, (lambda e, d=dst, sr=src: e.dma_start(d, sr)),
                waits=w, inc=psem, n=16, d=80.0)
            CUR[0][stream][-1][7] = nbytes * 0.01

        def flush_ch2():
            for k in "TVSYG":
                OPS[k].extend(CH2[k])
                CH2[k] = []

        ck = {
            'xcast_end': 0, 'wqb_rd': 0, 'wob_rd': 0, 'wdb_rd': 0, 'wgub_rd': 0,
            'kv_pe_end': 0, 'had_end': 0, 'avn_h1': 0,
            'erep_end': 0, 'wo_stt_end': 0, 'avtt_end': [0, 0], 'q_evac_last': 0,
            'musq1': 0, 'nm1': 0, 'musq2': 0, 'vpe2': 0, 'ffn': None,
            'nmb1_v': 0, 'nmb2_v': 0, 'wo_free': {0: None, 1: None},
            'vpesub1': 0, 'vpesub2': 0,
        }
        xc_store = {}

        def plan_head(s):
            t0 = s * ST
            xb = x_bf[s % 2]
            xcast_c = []
            for c in range(XC):
                w = []
                wfree = ck['wo_free'][s % 2]
                if wfree is not None:
                    w.append((s_v, wfree[c]))
                dst = xb[:, c * ST:(c + 1) * ST]
                src = xt[c * 128:(c + 1) * 128, t0:t0 + ST]
                w = [(s_x[c % 4], s_x[c % 4].c)] + w
                add('G', (lambda e, d=dst, sr=src: e.dma_start(d, sr)),
                    waits=w, inc=s_x[c % 4], n=16)
                CUR[0]['G'][-1][7] = 262144 * 0.01
                xcast_c.append((s_x[c % 4], s_x[c % 4].c))
            xc_store[s] = xcast_c

            pe0 = s_pe.c
            for m in range(2):
                for c in range(XC):
                    w = []
                    if m == 0:
                        w.append(xcast_c[c])
                        if c == 0:
                            if ck['avtt_end'][1]:
                                w.append((s_v, ck['avtt_end'][1]))
                    o = P[5 + m][:, :]
                    l = wd_sb[:, m * 1024 + c * 128:m * 1024 + (c + 1) * 128]
                    r = xb[:, c * ST:(c + 1) * ST]
                    add('T', (lambda e, o=o, l=l, r=r, st=(c == 0), sp=(c == XC - 1):
                              e.matmul(o, l, r, start=st, stop=sp)),
                        waits=w, inc=(s_pe if c == XC - 1 else None))
            lat_evac = []
            for m in range(2):
                w = [(s_pe, pe0 + m + 1)]
                if m == 0 and ck['kv_pe_end']:
                    w.append((s_pe, ck['kv_pe_end']))
                o, i_ = lat_bf[:, m * ST:(m + 1) * ST], P[5 + m][:, :]
                b = bdn_sb[:, m:m + 1]
                lat_evac.append(add('S', (lambda e, o=o, i=i_, b=b:
                                          e.activation(o, i, AF.Identity, bias=b)),
                                    waits=w, inc=s_s))
            lat_done = s_s.c

            ld0 = [s_ld[0].c, s_ld[1].c]
            pe0, sc0 = s_pe.c, s_s.c
            for m in range(XC):
                w = []
                if m < 2:
                    if ck['wqb_rd']:
                        w.append((s_pe, ck['wqb_rd']))
                else:
                    w.append((s_pe, pe0 + m - 1))
                dst, src = wqb[m % 2][:, :], wq_s[:, m * 1024:(m + 1) * 1024]
                dma('Y', s_ld[m % 2], dst, src, w)
            for m in range(XC):
                w = [(s_ld[m % 2], ld0[m % 2] + 16 * (m // 2 + 1))]
                w.append((s_s, lat_done if m < 2 else sc0 + m - 1))
                for c in range(XC):
                    o = P[5 + m % 2][:, :]
                    l = wqb[m % 2][:, c * 128:(c + 1) * 128]
                    r = xb[:, c * ST:(c + 1) * ST]
                    add('T', (lambda e, o=o, l=l, r=r, st=(c == 0), sp=(c == XC - 1):
                              e.matmul(o, l, r, start=st, stop=sp)),
                        waits=(w if c == 0 else ()), inc=(s_pe if c == XC - 1 else None))
            ck['wqb_rd'] = s_pe.c
            q_evac_v = []
            for m in range(XC):
                w = [(s_pe, pe0 + m + 1)]
                if m == 0 and ck['had_end']:
                    w.append((s_v, ck['had_end']))
                o, i_ = q_sb[:, m * ST:(m + 1) * ST], P[5 + m % 2][:, :]
                b = bq_sb[:, m:m + 1]
                q_evac_v.append(add('S', (lambda e, o=o, i=i_, b=b:
                                          e.activation(o, i, AF.Identity, bias=b)),
                                   waits=w, inc=s_s))
            ck['q_evac_last'] = q_evac_v[-1]
            return q_evac_v[-1]

        q_done = plan_head(0)

        def plan_ffn_ch2(s):
            t0 = s * ST
            f = ck['ffn']
            pe_b2, v_b2, s_b2 = f['pe2_prev'], f['v2_prev'], f['s2_prev']
            gd0 = [s_gd[0].c, s_gd[1].c]
            pe0, v0, sc0 = s_pe2.c, s_v2.c, s_s2.c
            for m in range(24):
                w = []
                if m < 2:
                    if ck['wgub_rd']:
                        w.append((s_pe2, ck['wgub_rd']))
                else:
                    w.append((s_pe2, pe0 + 2 * (m - 1)))
                d1, r1s = wgb[m % 2][:, :], wg_s[:, m * 1024:(m + 1) * 1024]
                d2, r2s = wub[m % 2][:, :], wu_s[:, m * 1024:(m + 1) * 1024]
                dma('G', s_gd[m % 2], d1, r1s, w)
                dma('G', s_gd[m % 2], d2, r2s)
            for m in range(24):
                w = [(s_gd[m % 2], gd0[m % 2] + 32 * (m // 2 + 1))]
                if m == 0:
                    w += [(s_pe2, pe_b2), (s_v2, v_b2), (s_s2, s_b2)]
                elif m == 1:
                    w.append((s_s2, s_b2))
                else:
                    w.append((s_s2, sc0 + m - 1))
                for c in range(XC):
                    o = P[0 if m % 2 == 0 else 4][:, :]
                    l = wgb[m % 2][:, c * 128:(c + 1) * 128]
                    r = x1_bf[:, c * ST:(c + 1) * ST]
                    ww = list(w) if c == 0 else []
                    if m == 0:
                        ww.append((s_v, f['x1_v'][c]))
                    add('T', (lambda e, o=o, l=l, r=r, st=(c == 0), sp=(c == XC - 1):
                              e.matmul(o, l, r, start=st, stop=sp)),
                        waits=ww, inc=(s_pe2 if c == XC - 1 else None))
                w2 = []
                if m >= 1:
                    w2.append((s_v2, v0 + m))
                for c in range(XC):
                    o = P[1][:, :]
                    l = wub[m % 2][:, c * 128:(c + 1) * 128]
                    r = x1_bf[:, c * ST:(c + 1) * ST]
                    add('T', (lambda e, o=o, l=l, r=r, st=(c == 0), sp=(c == XC - 1):
                              e.matmul(o, l, r, start=st, stop=sp)),
                        waits=(w2 if c == 0 else ()), inc=(s_pe2 if c == XC - 1 else None))
            ck['wgub_rd'] = s_pe2.c
            for m in range(24):
                w = [(s_pe2, pe0 + 2 * m + 1)]
                if m >= 2:
                    w.append((s_v2, v0 + m - 1))
                o, i_ = s_sil[m % 2][:, :], P[0 if m % 2 == 0 else 4][:, :]
                add('S', (lambda e, o=o, i=i_:
                          e.activation(o, i, AF.Silu, scale=1.0 / 127.0)),
                    waits=w, inc=s_s2, d=800)
            h_v = []
            for m in range(24):
                w = [(s_s2, sc0 + m + 1), (s_pe2, pe0 + 2 * m + 2)]
                o = h_sb[:, m * 512:(m + 1) * 512]
                a, b = s_sil[m % 2][:, :], P[1][:, :]
                h_v.append(add('V', (lambda e, o=o, a=a, b=b: e.tensor_mul(o, a, b)),
                               waits=w, inc=s_v2, d=780))

            ld0 = [s_ld2[0].c, s_ld2[1].c]
            pe1, v1 = s_pe2.c, s_v2.c
            for mo in range(XC):
                w = []
                if mo < 2:
                    if ck['wdb_rd']:
                        w.append((s_pe2, ck['wdb_rd']))
                else:
                    w.append((s_pe2, pe1 + mo - 1))
                dst, src = wdb[mo % 2][:, :], wdn_s[:, mo * 3072:(mo + 1) * 3072]
                dma('Y', s_ld2[mo % 2], dst, src, w, nbytes=786432)
            r2_v = []
            for mo in range(XC):
                w = [(s_ld2[mo % 2], ld0[mo % 2] + 16 * (mo // 2 + 1))]
                if mo >= 1:
                    w.append((s_v2, v1 + mo))
                for hc in range(24):
                    o = P[2][:, :]
                    l = wdb[mo % 2][:, hc * 128:(hc + 1) * 128]
                    r = h_sb[:, hc * 512:(hc + 1) * 512]
                    ww = list(w) if hc == 0 else []
                    if mo == 0:
                        ww.append((s_v2, h_v[hc]))
                    add('T', (lambda e, o=o, l=l, r=r, st=(hc == 0), sp=(hc == 23):
                              e.matmul(o, l, r, start=st, stop=sp)),
                        waits=ww, inc=(s_pe2 if hc == 23 else None))
            ck['wdb_rd'] = s_pe2.c
            for mo in range(XC):
                w = [(s_pe2, pe1 + mo + 1)]
                o = r2_bf[:, mo * ST:(mo + 1) * ST]
                i_ = P[2][:, :]
                x2_ = x1_bf[:, mo * ST:(mo + 1) * ST]
                r2_v.append(add('V', (lambda e, o=o, i=i_, x=x2_:
                                      e.scalar_tensor_tensor(o, i, SCL_FF, x,
                                                             ALU.mult, ALU.add)),
                                waits=w, inc=s_v2, d=810))

            pe2, v2, sc2 = s_pe2.c, s_v2.c, s_s2.c
            for J in range(XC):
                w = [(s_v2, r2_v[J])]
                if J >= 2:
                    w.append((s_pe2, pe2 + J - 1))
                o, i_ = sqb2[J % 2][:, :], r2_bf[:, J * ST:(J + 1) * ST]
                add('S', (lambda e, o=o, i=i_: e.activation(o, i, AF.Square)),
                    waits=w, inc=s_s2, d=850)
            for J in range(XC):
                w = [(s_s2, sc2 + J + 1)]
                if J == 0:
                    w.append((s_s, ck['musq1']))
                o2, r2 = P[3][:, :], sqb2[J % 2][:, :]
                add('T', (lambda e, o=o2, r=r2, st=(J == 0), sp=(J == XC - 1):
                          e.matmul(o, ones_sb[:, :], r, start=st, stop=sp)),
                    waits=w, inc=s_pe2)
            ex2_val = add('S', (lambda e, o=vpe2[:, :], i=P[3][:, :]:
                                e.activation(o, i, AF.Identity,
                                             scale=1.0 / 1024.0)),
                          waits=[(s_pe2, pe2 + XC), (s_v2, ck['vpesub2'])],
                          inc=s_s2)
            for J in range(XC):
                w = [(s_v2, r2_v[J])]
                if J == 0:
                    w.append((s_s2, ex2_val))
                o, r = P[3][:, :], r2_bf[:, J * ST:(J + 1) * ST]
                add('T', (lambda e, o=o, r=r, st=(J == 0), sp=(J == XC - 1):
                          e.matmul(o, ones_sb[:, :], r, start=st, stop=sp)),
                    waits=w, inc=s_pe2)
            mean2_val = add('S', (lambda e, o=meanb2[:, :], i=P[3][:, :]:
                                  e.activation(o, i, AF.Identity,
                                               scale=1.0 / 1024.0)),
                            waits=[(s_pe2, pe2 + 2 * XC), (s_v2, ck['nmb2_v'])],
                            inc=s_s2)
            musq_val = add('S', (lambda e, o=musq2[:, :], i=meanb2[:, :]:
                                 e.activation(o, i, AF.Square)),
                           waits=[(s_s2, mean2_val)], inc=s_s2)
            vpe2_val = add('V', (lambda e, o=vpe2[:, :], a=vpe2[:, :],
                                 b=musq2[:, :]: e.tensor_sub(o, a, b)),
                           waits=[(s_s2, musq_val), (s_v2, s_v2.c)],
                           inc=s_v2, d=640)
            ck['vpesub2'] = vpe2_val
            add('S', (lambda e, o=lnscr2[:, :], i=vpe2[:, :], b=cvec_sb[:, 1:2]:
                      e.activation(o, i, AF.Ln, bias=b)),
                waits=[(s_v2, vpe2_val)], inc=s_s2)
            rstd2_val = add('S', (lambda e, o=rstd2[:, :], i=lnscr2[:, :]:
                                  e.activation(o, i, AF.Exp, scale=-0.5)),
                            waits=[(s_s2, s_s2.c)], inc=s_s2)
            ck['nmb2_v'] = add('V', (lambda e, o=nmb2[:, :], i=meanb2[:, :],
                                     r=rstd2[:, :]:
                                     e.scalar_tensor_tensor(o, i, -1.0, r,
                                                            ALU.mult, ALU.mult)),
                               waits=[(s_s2, rstd2_val), (s_v2, s_v2.c)],
                               inc=s_v2, d=830)
            outb = [s_out[0].c, s_out[1].c]
            x2_v = []
            for J in range(XC):
                add('V', (lambda e, o=lnscr2[:, :], a=r2_bf[:, J * ST:(J + 1) * ST],
                          b=rstd2[:, :]: e.tensor_mul(o, a, b)),
                    waits=[(s_v2, s_v2.c)], inc=s_v2, d=830)
                w = [(s_v2, s_v2.c)]
                val = outb[J % 2] + 16 * (J // 2) if J >= 2 else outb[J % 2]
                if val > 0:
                    w.append((s_out[J % 2], val))
                o = x2b[J % 2][:, :]
                x2_v.append(add('V', (lambda e, o=o, a=lnscr2[:, :], b=nmb2[:, :]:
                                      e.tensor_add(o, a, b)),
                                waits=w, inc=s_v2, d=830))
            for J in range(XC):
                w = [(s_v2, x2_v[J])]
                dst = yt[J * 128:(J + 1) * 128, t0:t0 + ST]
                src = x2b[J % 2][:, :]
                dma('Y', s_out[J % 2], dst, src, w)
            ck['musq2'] = mean2_val

        for s in range(nst):
            if s >= 1:
                CUR[0] = CH2
                plan_ffn_ch2(s - 1)
                CUR[0] = OPS

            pe_b, v_b, s_b = ck.get('bar') or (s_pe.c, s_v.c, s_s.c)
            first_T, first_V, first_S = [True], [True], [True]

            def wT(extra=()):
                if first_T[0]:
                    first_T[0] = False
                    return ((s_v, v_b), (s_s, s_b)) + tuple(extra)
                return tuple(extra)

            def wV(extra=()):
                if first_V[0]:
                    first_V[0] = False
                    return ((s_pe, pe_b), (s_s, s_b), (s_v, v_b)) + tuple(extra)
                return tuple(extra)

            def wS(extra=()):
                if first_S[0]:
                    first_S[0] = False
                    return ((s_pe, pe_b), (s_v, v_b), (s_s, s_b)) + tuple(extra)
                return tuple(extra)

            prev_red_end = 0
            prev_av_pe_end = 0
            prev_avn_end = 0
            prev_esum_v = 0
            prev_wout_v = [0] * XC
            prev_avn_end_hf0 = 0

            def plan_kv_part(hf, part, guard=()):
                pe0, sc0 = s_pe.c, s_s.c
                wgt = wupk_sb
                for g in range(NH):
                    w = []
                    if g < 2:
                        w += list(wT([]))
                        w += list(guard)
                    if sc0 + g - 1 > 0:
                        w.append((s_s, sc0 + g - 1))
                    for c in range(2):
                        o = P[5 + g % 2][:, 0:HF]
                        l = wgt[:, g * 256 + c * 128:g * 256 + (c + 1) * 128]
                        r = lat_bf[:, c * ST + hf * HF:c * ST + hf * HF + HF]
                        add('T', (lambda e, o=o, l=l, r=r, st=(c == 0), sp=(c == 1):
                                  e.matmul(o, l, r, start=st, stop=sp)),
                            waits=(w if c == 0 else ()), inc=(s_pe if c == 1 else None), d=152)
                evs = []
                for g in range(NH):
                    dst = krep
                    bias = bupk_sb
                    w = list(wS([(s_pe, pe0 + g + 1)]))
                    if g == 0:
                        if prev_red_end:
                            w.append((s_pe, prev_red_end))
                    o, i_ = dst[:, g * HF:(g + 1) * HF], P[5 + g % 2][:, 0:HF]
                    b = bias[:, g:g + 1]
                    evs.append(add('S', (lambda e, o=o, i=i_, b=b:
                                         e.activation(o, i, AF.Identity, bias=b)),
                                   waits=tuple(w), inc=s_s, d=560))
                return evs

            def plan_vd():
                pe0, sc0 = s_pe.c, s_s.c
                for g in range(NH):
                    w = []
                    if sc0 + g - 1 > 0:
                        w.append((s_s, sc0 + g - 1))
                    for c in range(2):
                        o = P[5 + g % 2][:, 0:ST]
                        l = wupv_sb[:, g * 256 + c * 128:g * 256 + (c + 1) * 128]
                        r = lat_bf[:, c * ST:(c + 1) * ST]
                        add('T', (lambda e, o=o, l=l, r=r, st=(c == 0), sp=(c == 1):
                                  e.matmul(o, l, r, start=st, stop=sp)),
                            waits=(w if c == 0 else ()),
                            inc=(s_pe if c == 1 else None), d=258)
                evs = []
                for g in range(NH):
                    w = list(wS([(s_pe, pe0 + g + 1)]))
                    if g == 0:
                        for h2 in range(2):
                            if ck['avtt_end'][h2]:
                                w.append((s_v, ck['avtt_end'][h2]))
                    o = vdup_all[:, g * ST:(g + 1) * ST]
                    i_ = P[5 + g % 2][:, 0:ST]
                    b = bupv_sb[:, g:g + 1]
                    evs.append(add('S', (lambda e, o=o, i=i_, b=b:
                                         e.activation(o, i, AF.Identity, bias=b)),
                                   waits=tuple(w), inc=s_s, d=770))
                return evs

            def plan_sc(hf, kv_evac_g):
                nonlocal prev_red_end
                h0 = hf * HF
                pe0, v0, sc0 = s_pe.c, s_v.c, s_s.c
                q3 = q_sb[0:128, 0:XC * ST].rearrange("p (J t) -> p J t", J=XC)[:, :, h0:h0 + HF]
                for g in range(NH):
                    w = list(wV([]))
                    w.append((s_s, kv_evac_g[g]))
                    if g < 2:
                        if prev_red_end:
                            w.append((s_pe, prev_red_end))
                    else:
                        w.append((s_pe, pe0 + XC * (g - 1)))
                    o = hadb[g % 2][:, :].rearrange("p (J t) -> p J t", J=XC)
                    b = krep[:, g * HF:(g + 1) * HF].unsqueeze(1).broadcast_to([128, XC, HF])
                    add('V', (lambda e, o=o, a=q3, b=b: e.tensor_mul(o, a, b)),
                        waits=tuple(w), inc=s_v, d=1460)
                had_last = s_v.c
                for g in range(NH):
                    w = list(wS([(s_pe, pe0 + XC * (g + 1))]))
                    if g == 0 and prev_av_pe_end:
                        w.append((s_pe, prev_av_pe_end))
                        w.append((s_v, prev_esum_v))
                    o = e_sb[:, g * HF:(g + 1) * HF]
                    i_ = P[5 + g % 2][0:16, 0:HF]
                    b = cvec_sb[0:16, 0:1]
                    add('S', (lambda e, o=o, i=i_, b=b:
                              e.activation(o, i, AF.Exp, bias=b, scale=0.125)),
                        waits=tuple(w), inc=s_s, d=560)
                for g in range(NH):
                    for J in range(XC):
                        idx = g * XC + J
                        w = list(wT([])) if idx == 0 else []
                        if J == 0:
                            w.append((s_v, v0 + g + 1))
                            w.append((s_s, sc0 + g - 1 if g >= 2 else sc0))
                        o = P[5 + g % 2][0:16, 0:HF]
                        l = selred_sb[:, J * 16:(J + 1) * 16]
                        r = hadb[g % 2][:, J * HF:(J + 1) * HF]
                        add('T', (lambda e, o=o, l=l, r=r, st=(J == 0), sp=(J == XC - 1):
                                  e.matmul(o, l, r, start=st, stop=sp)),
                            waits=tuple(w), inc=s_pe, d=152)
                prev_red_end = s_pe.c
                return s_s.c, had_last

            def plan_av(hf, exp_end, kv_evac_g, guard=()):
                nonlocal prev_av_pe_end, prev_avn_end, prev_esum_v, prev_avn_end_hf0
                pe0, v0, sc0 = s_pe.c, s_v.c, s_s.c
                pe_bcast = lambda J, gp: pe0 + J * 24 + (gp + 1 if gp < 3 else 3 * (gp - 1))

                def pe_gsum(J, g):
                    m = g // 2
                    return pe0 + J * 24 + (3 * m + 4 + (g % 2) if g < 12 else 21 + (g - 12))
                NPRE = 25
                v_norm0 = v0 + 17
                v_norm = v0 + NPRE
                v_tt = lambda J, gp: v0 + NPRE + J * 8 + gp + 1
                avn_s = lambda J: sc0 + J + 1

                exp_g = lambda g: exp_end - NH + g + 1
                add('V', (lambda e, o=esum32[:, :], a=e_sb[:, 0:HF],
                          b=e_sb[:, HF:2 * HF]: e.tensor_tensor(o, a, b, ALU.add)),
                    waits=wV([(s_s, exp_g(1)), (s_v, s_v.c)]), inc=s_v, d=420)
                for g in range(2, NH):
                    a2 = e_sb[:, g * HF:(g + 1) * HF]
                    add('V', (lambda e, o=esum32[:, :], a=esum32[:, :], b=a2:
                              e.tensor_tensor(o, a, b, ALU.add)),
                        waits=[(s_s, exp_g(g)), (s_v, s_v.c)], inc=s_v, d=420)
                add('V', (lambda e, o=zinv[:, :], i=esum32[:, :]: e.reciprocal(o, i)),
                    waits=[(s_v, s_v.c)], inc=s_v, d=2100)
                add('V', (lambda e, o=zinvb[:, :], i=zinv[:, :]: e.tensor_copy(o, i)),
                    waits=[(s_v, s_v.c)], inc=s_v, d=370)
                for gp in range(8):
                    e3 = e_sb[:, gp * 512:(gp + 1) * 512].rearrange(
                        "p (g t) -> p g t", g=2)
                    zb = zinvb[0:16, 0:HF].unsqueeze(1).broadcast_to([16, 2, HF])
                    add('V', (lambda e, o=e3, a=e3, b=zb: e.tensor_mul(o, a, b)),
                        waits=[(s_v, s_v.c)], inc=s_v, d=440)
                assert s_v.c == v0 + NPRE
                prev_esum_v = s_v.c
                for J in range(XC):
                    for gp in range(8):
                        w = [(s_pe, pe_bcast(J, gp))]
                        if J == 0:
                            w.append((s_s, kv_evac_g[2 * gp + 1]))
                        if gp >= 2:
                            w.append((s_pe, pe_gsum(J, 2 * gp - 3)))
                        elif J >= 1:
                            w.append((s_pe, pe_gsum(J - 1, 13 + 2 * gp)))
                        o = p_sb[gp % 2][:, :].rearrange("p (g t) -> p g t", g=2)
                        a = P[5 + gp % 2][:, :].rearrange("p (g t) -> p g t", g=2)
                        b = vdup_all[0:128, 0:NH * ST].rearrange(
                            "p (g t) -> p g t", g=NH)[:, 2 * gp:2 * gp + 2,
                                                      hf * HF:hf * HF + HF]
                        add('V', (lambda e, o=o, a=a, b=b: e.tensor_mul(o, a, b)),
                            waits=w, inc=s_v, d=640)
                    o = avn[:, J * HF:(J + 1) * HF]
                    i_ = P[7][:, 0:HF]
                    add('S', (lambda e, o=o, i=i_:
                              e.activation(o, i, AF.Identity)),
                        waits=[(s_pe, pe_gsum(J, NH - 1)), (s_s, s_s.c)],
                        inc=s_s, d=510)
                ck['avtt_end'][hf] = v_tt(XC - 1, 7)
                prev_avn_end = s_s.c
                erep_end = v_tt(XC - 1, 7)
                for J in range(XC):
                    sched = [('b', 0), ('b', 1), ('b', 2), ('g', 0), ('g', 1)]
                    for gp in range(3, 8):
                        sched += [('b', gp), ('g', 2 * (gp - 2)), ('g', 2 * (gp - 2) + 1)]
                    sched += [('g', 12), ('g', 13), ('g', 14), ('g', 15)]
                    first = True
                    for kind, idx2 in sched:
                        w = []
                        if first:
                            first = False
                            if J == 0:
                                w += list(wT([]))
                                w.append((s_s, exp_end))
                                w += list(guard)
                        if kind == 'b' and J == 0:
                            w.append((s_v, v_norm0 + idx2 + 1))
                        if kind == 'b':
                            gp = idx2
                            lin = J * 8 + gp
                            if lin >= 2:
                                J2, gp2 = divmod(lin - 2, 8)
                                w.append((s_v, v_tt(J2, gp2)))
                            o = P[5 + gp % 2][:, :]
                            l = selz_sb[:, J * 128:(J + 1) * 128]
                            r = e_sb[:, gp * 512:(gp + 1) * 512]
                            add('T', (lambda e, o=o, l=l, r=r:
                                      e.matmul(o, l, r, start=True, stop=True)),
                                waits=w, inc=s_pe)
                        else:
                            g = idx2
                            if g % 2 == 0:
                                w.append((s_v, v_tt(J, g // 2)))
                            if g == 0:
                                if J >= 1:
                                    w.append((s_s, avn_s(J - 1)))
                                elif hf == 0:
                                    w.append((s_s, kv_evac_g[-1]))
                                else:
                                    w.append((s_s, prev_avn_end_hf0))
                            o = P[7][:, 0:HF]
                            r = p_sb[(g // 2) % 2][:, (g % 2) * HF:(g % 2 + 1) * HF]
                            add('T', (lambda e, o=o, r=r, st=(g == 0), sp=(g == NH - 1):
                                      e.matmul(o, id_sb[:, :], r, start=st, stop=sp)),
                                waits=w, inc=s_pe, d=152)
                assert s_pe.c == pe0 + XC * 24
                assert s_v.c == v0 + NPRE + XC * 8
                prev_av_pe_end = s_pe.c
                if hf == 0:
                    prev_avn_end_hf0 = prev_avn_end
                return prev_avn_end, erep_end

            def plan_wo(hf, avn_end, p56_guard):
                nonlocal prev_wout_v
                h0 = hf * HF
                ld0 = [s_ld[0].c, s_ld[1].c]
                pe0, v0 = s_pe.c, s_v.c
                for m in range(XC):
                    w = []
                    if m < 2:
                        if ck['wob_rd']:
                            w.append((s_pe, ck['wob_rd']))
                    else:
                        w.append((s_pe, pe0 + m - 1))
                    dst, src = wob[m % 2][:, :], wout_s[:, m * 1024:(m + 1) * 1024]
                    dma('Y', s_ld[m % 2], dst, src, w)
                for m in range(XC):
                    w = [(s_ld[m % 2], ld0[m % 2] + 16 * (m // 2 + 1))]
                    if m == 0:
                        w.append((s_s, avn_end))
                        w += p56_guard
                    if m < 2:
                        if prev_wout_v[0]:
                            w.append((s_v, prev_wout_v[XC - 2 + m]))
                    if m >= 2:
                        w.append((s_v, v0 + m - 1))
                    for J in range(XC):
                        o = P[5 + m % 2][:, 0:HF]
                        l = wob[m % 2][:, J * 128:(J + 1) * 128]
                        r = avn[:, J * HF:(J + 1) * HF]
                        add('T', (lambda e, o=o, l=l, r=r, st=(J == 0), sp=(J == XC - 1):
                                  e.matmul(o, l, r, start=st, stop=sp)),
                            waits=(w if J == 0 else ()), inc=(s_pe if J == XC - 1 else None))
                ck['wob_rd'] = s_pe.c
                wout_v = []
                for m in range(XC):
                    w = list(wV([(s_pe, pe0 + m + 1)]))
                    if m == 0 and hf == 0 and ck['musq1']:
                        w.append((s_s, ck['musq1']))
                    o = r1_bf[:, m * ST + h0:m * ST + h0 + HF]
                    i_ = P[5 + m % 2][:, 0:HF]
                    b = bout_sb[:, m:m + 1]
                    x2_ = x_bf[s % 2][:, m * ST + h0:m * ST + h0 + HF]
                    wout_v.append(add('V', (lambda e, o=o, i=i_, b=b, x=x2_:
                                            e.scalar_tensor_tensor(o, i, b, x,
                                                                   ALU.add, ALU.add)),
                                      waits=w, inc=s_v))
                prev_wout_v = wout_v
                return wout_v

            kd0 = plan_kv_part(0, 0)
            vd_all = plan_vd()
            exp_end0, had0 = plan_sc(0, kd0)
            avn_end0, erep_end0 = plan_av(0, exp_end0, vd_all)
            kd1 = plan_kv_part(1, 0, guard=[(s_v, ck['avtt_end'][0])])
            exp_end1, had1 = plan_sc(1, kd1)
            wo0 = plan_wo(0, avn_end0, [(s_s, exp_end1)])
            avn_end1, erep_end1 = plan_av(1, exp_end1, vd_all, [(s_v, wo0[-1])])
            wo1 = plan_wo(1, avn_end1, [(s_v, erep_end1)])
            ck['kv_pe_end'] = s_pe.c
            ck['had_end'] = had1
            ck['avn_h1'] = avn_end1
            ck['erep_end'] = erep_end1
            ck['wo_stt_end'] = wo1[-1]
            ck['wo_free'][s % 2] = [max(a, b) for a, b in zip(wo0, wo1)]

            flush_ch2()

            ck['bar'] = (s_pe.c, s_v.c, s_s.c)

            pe0, v0, sc0 = s_pe.c, s_v.c, s_s.c
            for J in range(XC):
                w = list(wS([]))
                w.append((s_v, wo1[J]))
                if J >= 2:
                    w.append((s_pe, pe0 + J - 1))
                elif ck['musq1']:
                    w.append((s_s, ck['musq1']))
                o, i_ = sqb[J % 2][:, :], r1_bf[:, J * ST:(J + 1) * ST]
                add('S', (lambda e, o=o, i=i_: e.activation(o, i, AF.Square)),
                    waits=tuple(w), inc=s_s, d=850)
            for J in range(XC):
                w = list(wT([]))
                w.append((s_s, sc0 + J + 1))
                if J == 0:
                    w.append((s_s2, ck['musq2']))
                o2, r2 = P[3][:, :], sqb[J % 2][:, :]
                add('T', (lambda e, o=o2, r=r2, st=(J == 0), sp=(J == XC - 1):
                          e.matmul(o, ones_sb[:, :], r, start=st, stop=sp)),
                    waits=tuple(w), inc=s_pe)
            ex1_val = add('S', (lambda e, o=vpe[:, :], i=P[3][:, :]:
                                e.activation(o, i, AF.Identity,
                                             scale=1.0 / 1024.0)),
                          waits=[(s_pe, pe0 + XC), (s_v, ck['vpesub1'])],
                          inc=s_s)
            for J in range(XC):
                w = [(s_v, wo1[J])]
                if J == 0:
                    w.append((s_s, ex1_val))
                o, r = P[3][:, :], r1_bf[:, J * ST:(J + 1) * ST]
                add('T', (lambda e, o=o, r=r, st=(J == 0), sp=(J == XC - 1):
                          e.matmul(o, ones_sb[:, :], r, start=st, stop=sp)),
                    waits=w, inc=s_pe)
            mean1_val = add('S', (lambda e, o=meanb1[:, :], i=P[3][:, :]:
                                  e.activation(o, i, AF.Identity,
                                               scale=1.0 / 1024.0)),
                            waits=[(s_pe, pe0 + 2 * XC), (s_v, ck['nmb1_v'])],
                            inc=s_s)
            musq_val = add('S', (lambda e, o=musq[:, :], i=meanb1[:, :]:
                                 e.activation(o, i, AF.Square)),
                           waits=[(s_s, mean1_val)], inc=s_s)
            vpe1_val = add('V', (lambda e, o=vpe[:, :], a=vpe[:, :],
                                 b=musq[:, :]: e.tensor_sub(o, a, b)),
                           waits=wV([(s_s, musq_val), (s_v, s_v.c)]),
                           inc=s_v, d=640)
            ck['vpesub1'] = vpe1_val
            add('S', (lambda e, o=lnscr[:, :], i=vpe[:, :], b=cvec_sb[:, 1:2]:
                      e.activation(o, i, AF.Ln, bias=b)),
                waits=[(s_v, vpe1_val)], inc=s_s)
            rstd1_val = add('S', (lambda e, o=rstd[:, :], i=lnscr[:, :]:
                                  e.activation(o, i, AF.Exp, scale=-0.5)),
                            waits=[(s_s, s_s.c)], inc=s_s)
            nmb1_v = add('V', (lambda e, o=nmb[:, :], i=meanb1[:, :],
                               r=rstd[:, :]:
                               e.scalar_tensor_tensor(o, i, -1.0, r,
                                                      ALU.mult, ALU.mult)),
                         waits=[(s_s, rstd1_val), (s_v, s_v.c)],
                         inc=s_v, d=830)
            ck['nmb1_v'] = nmb1_v
            ck['musq1'] = mean1_val
            x1_v = []
            for J in range(XC):
                add('V', (lambda e, o=lnscr[:, :], a=r1_bf[:, J * ST:(J + 1) * ST],
                          b=rstd[:, :]: e.tensor_mul(o, a, b)),
                    waits=[(s_v, s_v.c), (s_v, nmb1_v)], inc=s_v, d=830)
                o = x1_bf[:, J * ST:(J + 1) * ST]
                x1_v.append(add('V', (lambda e, o=o, a=lnscr[:, :], b=nmb[:, :]:
                                      e.tensor_add(o, a, b)),
                                waits=[(s_v, s_v.c)], inc=s_v, d=830))
            ck['nm1'] = vpe1_val

            ck['ffn'] = dict(x1_v=x1_v, pe2_prev=s_pe2.c, v2_prev=s_v2.c,
                             s2_prev=s_s2.c)

            if s + 1 < nst:
                plan_head(s + 1)

        CUR[0] = CH2
        plan_ffn_ch2(nst - 1)
        CUR[0] = OPS
        flush_ch2()

        add('Y', (lambda e: e.nop()),
            waits=[(s_out[0], s_out[0].c), (s_out[1], s_out[1].c)], d=30)

        nodes = [nd for k in "TVSYG" for nd in OPS[k]]
        nid = {id(nd): nd for nd in nodes}
        chains = {}
        for k in "TVSYG":
            for nd in OPS[k]:
                chains.setdefault((k, nd[4]), []).append(nd)
        preds = {id(nd): [] for nd in nodes}
        for nd in nodes:
            for sem, val in nd[0]:
                assert val <= sem.c, (nd[5], nd[4], sem.h.name, val, sem.c)
                t = SEMOWN.get((id(sem), val))
                if t is not None and id(t) in nid and t is not nd:
                    preds[id(nd)].append(t)
        succs = {id(nd): [] for nd in nodes}
        indeg = {id(nd): 0 for nd in nodes}
        for nd in nodes:
            for p in preds[id(nd)]:
                succs[id(p)].append(nd); indeg[id(nd)] += 1
        for lst in chains.values():
            for a, b in zip(lst, lst[1:]):
                succs[id(a)].append(b); indeg[id(b)] += 1
        topo = [nd for nd in nodes if indeg[id(nd)] == 0]
        qi = 0
        while qi < len(topo):
            nd = topo[qi]; qi += 1
            for sc in succs[id(nd)]:
                indeg[id(sc)] -= 1
                if indeg[id(sc)] == 0:
                    topo.append(sc)
        if len(topo) != len(nodes):
            done = {id(nd) for nd in topo}
            msgs = []
            for k in "TVSYG":
                cnt_k = 0
                for ci, nd in enumerate(OPS[k]):
                    if id(nd) in done or cnt_k > 3:
                        continue
                    cnt_k += 1
                    unres = [(sm.h.name, v) for sm, v in nd[0]
                             if (t := SEMOWN.get((id(sm), v))) is not None
                             and id(t) in nid and id(t) not in done]
                    msgs.append((k, nd[4], ci, unres))
            raise RuntimeError(
                f"dep cycle: {len(nodes)-len(topo)} stuck; first: {msgs}")
        CP = {id(nd): 0.0 for nd in nodes}
        for nd in reversed(topo):
            m = 0.0
            for sc in succs[id(nd)]:
                if CP[id(sc)] > m:
                    m = CP[id(sc)]
            CP[id(nd)] = m + nd[6] + nd[7]
        qfree = {k: 0.0 for k in "TVSYG"}
        cptr = {key: 0 for key in chains}
        finish = {}
        sched = {k: [] for k in "TVSYG"}
        todo = len(nodes)
        while todo:
            best = None
            for (k, ch), lst in chains.items():
                i = cptr[(k, ch)]
                if i >= len(lst):
                    continue
                nd = lst[i]
                es = qfree[k]
                ok = True
                for p in preds[id(nd)]:
                    f = finish.get(id(p))
                    if f is None:
                        ok = False
                        break
                    if f > es:
                        es = f
                if not ok:
                    continue
                key = (round(es / 400.0), -CP[id(nd)], es)
                if best is None or key < best[0]:
                    best = (key, k, ch, nd, es)
            if best is None:
                raise RuntimeError("scheduler deadlock")
            _, k, ch, nd, es = best
            qfree[k] = es + nd[6]
            finish[id(nd)] = es + nd[6] + nd[7]
            sched[k].append(nd)
            cptr[(k, ch)] += 1
            todo -= 1
        all_sems = (s_ld + s_ld2 + s_x + s_gd + s_st + s_out +
                    [s_pe, s_v, s_s, s_pe2, s_v2, s_s2])
        cnt = {id(sm): SEM_BASE[id(sm)] for sm in all_sems}
        semq = {}
        newval = {}
        for k in "TVSYG":
            for nd in sched[k]:
                if nd[2] is not None:
                    sid = id(nd[2])
                    assert semq.setdefault(sid, k) == k, "sem on two queues"
                    cnt[sid] += nd[3]
                    newval[id(nd)] = cnt[sid]
        for nd in nodes:
            nw = {}
            for sem, val in nd[0]:
                t = SEMOWN.get((id(sem), val))
                v2 = newval[id(t)] if (t is not None and id(t) in nid) else val
                if nw.get(id(sem), (None, -1))[1] < v2:
                    nw[id(sem)] = (sem, v2)
            nd[0] = tuple(nw.values())

        with nc.Block() as blk:
            def runner(ops_list):
                def body(eng):
                    for waits, fn, incsem, n, _ch, _st, _d, _x in ops_list:
                        for sem, val in waits:
                            eng.wait_ge(sem.h, val)
                        inst = fn(eng)
                        if incsem is not None:
                            inst.then_inc(incsem.h, n)
                return body
            blk.tensor(runner(sched['T']))
            blk.vector(runner(sched['V']))
            blk.scalar(runner(sched['S']))
            blk.sync(runner(sched['Y']))
            blk.gpsimd(runner(sched['G']))

    return nc



def _lhsT_pieces(W):
    K, M = W.shape
    kc, mt = K // 128, M // 128
    return np.ascontiguousarray(
        W.reshape(kc, 128, mt, 128).transpose(1, 2, 0, 3).reshape(128, mt * kc * 128))


def kernel(L0_attn_x, L0_attn_Wq, L0_attn_bq, L0_attn_Wdown, L0_attn_bdown,
           L0_attn_Wup, L0_attn_bup, L0_attn_Wout, L0_attn_bout,
           L0_ffn_w_gate, L0_ffn_w_up, L0_ffn_w_down):
    x = np.asarray(L0_attn_x, dtype=np.float32).reshape(-1, D)
    Wq = np.asarray(L0_attn_Wq, dtype=np.float32)
    bq = np.asarray(L0_attn_bq, dtype=np.float32)
    Wd = np.asarray(L0_attn_Wdown, dtype=np.float32)
    bd = np.asarray(L0_attn_bdown, dtype=np.float32)
    Wu = np.asarray(L0_attn_Wup, dtype=np.float32)
    bu = np.asarray(L0_attn_bup, dtype=np.float32)
    Wo = np.asarray(L0_attn_Wout, dtype=np.float32)
    bo = np.asarray(L0_attn_bout, dtype=np.float32)
    wg = np.asarray(L0_ffn_w_gate, dtype=np.int32)
    wu = np.asarray(L0_ffn_w_up, dtype=np.int32)
    wdn = np.asarray(L0_ffn_w_down, dtype=np.int32)

    Wu3 = Wu.reshape(DL, NH, 2 * HD)
    kc_ = Wu3[:, :, :HD]
    vc_ = Wu3[:, :, HD:]
    wupk_dup = np.concatenate([kc_, kc_], axis=2).reshape(DL, NH * 128)
    wupv_dup = np.concatenate([vc_, vc_], axis=2).reshape(DL, NH * 128)

    buk = bu.reshape(NH, 2 * HD)[:, :HD]
    buv = bu.reshape(NH, 2 * HD)[:, HD:]
    bupk_h = np.ascontiguousarray(np.concatenate([buk, buk], axis=1).T)
    bupv_h = np.ascontiguousarray(np.concatenate([buv, buv], axis=1).T)

    selred = np.zeros((128, 128), dtype=np.float32)
    for J in range(XC):
        for p in range(128):
            selred[p, J * 16 + 2 * J + p // 64] = 1.0
    selz = np.zeros((16, 1024), dtype=np.float32)
    for J in range(XC):
        for hh in range(2):
            selz[2 * J + hh, J * 128 + hh * HD:J * 128 + (hh + 1) * HD] = 1.0
    cvec = np.zeros((128, 2), dtype=np.float32)
    cvec[:, 0] = -EXPC
    cvec[:, 1] = EPS

    common = dict(
        wq=_lhsT_pieces(Wq), wd=_lhsT_pieces(Wd),
        wupk=_lhsT_pieces(wupk_dup), wupv=_lhsT_pieces(wupv_dup),
        wout=_lhsT_pieces(Wo),
        wg=_lhsT_pieces(np.ascontiguousarray(wg.T)),
        wu=_lhsT_pieces(np.ascontiguousarray(wu.T)),
        wdn=_lhsT_pieces(np.ascontiguousarray(wdn.T)),
        bq=np.ascontiguousarray(bq.reshape(XC, 128).T),
        bdn=np.ascontiguousarray(bd.reshape(2, 128).T),
        bupk=bupk_h, bupv=bupv_h,
        bout=np.ascontiguousarray(bo.reshape(XC, 128).T),
        selred=selred, selz=selz, cvec=cvec,
        onesm=np.ones((128, 128), dtype=np.float32),
        idm=np.eye(128, dtype=np.float32),
    )
    in_maps = []
    for i in range(NCORES):
        m = dict(common)
        m["xt"] = np.ascontiguousarray(x[i * TPC:(i + 1) * TPC, :].T)
        in_maps.append(m)

    nc = build_nc()
    res = run_bass_kernel_spmd(nc, in_maps, core_ids=list(range(NCORES)))
    out = np.empty((NCORES * TPC, D), dtype=np.float32)
    for i in range(NCORES):
        out[i * TPC:(i + 1) * TPC, :] = res.results[i]["yt"].T
    return out.reshape(4, 8192, D)
